# version 01
# baseline (speedup 1.0000x reference)
import dataclasses
import numpy as np
import concourse.bass as bass
import concourse.mybir as mybir

F32 = mybir.dt.float32
BF16 = mybir.dt.bfloat16
U8 = mybir.dt.uint8
AF = mybir.ActivationFunctionType
ALU = mybir.AluOpType
AX = mybir.AxisListType.X

NH = 8
DQW = 128
DVW = 256
TC = 128
T = 1024
D = 2048
KT = D // 128
NCH = T // TC
EPS = 1e-6
LNEPS = 1e-5
SCALE_Q = float(DQW) ** -0.5
FFI = 8192
NEG = -1e30


def rep(ap, newap, doffel=0):
    return dataclasses.replace(ap, ap=newap, offset=ap.offset + doffel)


class Eng:
    def __init__(self, mk, name, semname):
        self.mk, self.name, self.semname = mk, name, semname
        self.q = []

    def op(self, f, inc=1, sem=None):
        s = sem or self.semname
        self.mk.cnt[s] = self.mk.cnt.get(s, 0) + inc
        tgt = self.mk.cnt[s]
        self.q.append(lambda e, f=f, s=s, inc=inc:
                      f(e).then_inc(self.mk.sems[s], inc))
        return (s, tgt)

    def raw(self, f):
        self.q.append(f)

    def wait(self, ev):
        if ev is None:
            return
        s, val = ev
        if val <= 0:
            return
        key = (self.name, s)
        if self.mk.hwm.get(key, 0) >= val:
            return
        self.mk.hwm[key] = val
        self.q.append(lambda e, s=s, val=val: e.wait_ge(self.mk.sems[s], val))

    def waits(self, evs):
        for ev in evs:
            self.wait(ev)

    def opb(self, f, inc=1, sem=None):
        ev = self.op(f, inc, sem)
        self.wait(ev)
        return ev


class Bank:
    def __init__(self, ap):
        self.ap = ap
        self.w_ev = None
        self.r_ev = None


class MK:
    def __init__(self, cfg):
        self.cfg = cfg
        self.nc = bass.Bass("TRN2", target_bir_lowering=False)
        self.cnt, self.hwm, self.sems = {}, {}, {}
        self._dbg_jobs = []
        self._ctx = []
        self._regions = {}

    def _enter(self, cm):
        v = cm.__enter__()
        self._ctx.append(cm)
        return v

    def evt(self, s):
        return (s, self.cnt.get(s, 0))

    def region(self, name, off, nbytes):
        self._regions[name] = (off, nbytes)

    def view(self, name, dtype, shape, byteoff=0):
        off, nbytes = self._regions[name]
        esz = mybir.dt.size(dtype)
        n_el = int(np.prod(shape[1:]))
        assert byteoff + n_el * esz <= nbytes, (name, shape, byteoff, nbytes)
        ap = self.pool[0:shape[0], off + byteoff: off + byteoff + n_el * esz]
        ap = ap.bitcast(dtype)
        if len(shape) == 3:
            ap = ap.rearrange("p (a b) -> p a b", b=shape[-1])
        return ap

    def tiles(self, name, dtype, n, width, byteoff=0):
        esz = mybir.dt.size(dtype)
        return [self.view(name, dtype, [128, width],
                          byteoff=byteoff + i * width * esz) for i in range(n)]

    def dram(self, name, shape, dtype, kind="ExternalInput", **kw):
        return self.nc.dram_tensor(name, list(shape), dtype, kind=kind, **kw)

    def dbg(self, name, src_ap, evs):
        if name not in self.cfg.get('dbg', ()):
            return
        t = self.dram("dbg_" + name, list(src_ap.shape), src_ap.dtype,
                      kind="ExternalOutput")
        self._dbg_jobs.append((t, src_ap, list(evs)))

    def mm_group(self, bank, mms, pre=()):
        te = self.E['tensor']
        te.wait(bank.r_ev)
        te.waits(pre)
        n = len(mms)
        for i, (osl, lh, rh) in enumerate(mms[:-1]):
            te.raw(lambda e, osl=osl, lh=lh, rh=rh, s=(i == 0):
                   e.matmul(osl, lh, rh, start=s, stop=False))
        osl, lh, rh = mms[-1]
        bank.w_ev = te.op(lambda e, osl=osl, lh=lh, rh=rh, s=(n == 1):
                          e.matmul(osl, lh, rh, start=s, stop=True))
        return bank.w_ev

    def evac(self, bank, eng, f, pre=()):
        en = self.E[eng]
        en.wait(bank.w_ev)
        en.waits(pre)
        ev = en.op(f)
        bank.r_ev = ev
        return ev

    def tr_one(self, src, dst, eng='vector', pre=(), dst_pre=(), bank=None,
               ident=None, scale=None):
        te = self.E['tensor']
        if bank is None:
            bank = self.trb[self._trq % 4]
            self._trq += 1
        P, W = src.shape
        te.wait(bank.r_ev)
        te.waits(pre)
        te.wait(self.ev_consts[0])
        idt = (ident or self.v['ident'])[0:src.shape[0], 0:src.shape[0]]
        bank.w_ev = te.op(lambda e, b=bank, s=src, i=idt:
                          e.transpose(b.ap[0:s.shape[1], 0:s.shape[0]], s, i))
        if scale is None:
            return self.evac(bank, eng,
                             lambda e, b=bank, d=dst, W=W, P=P:
                             e.tensor_copy(d, b.ap[0:W, 0:P]), pre=dst_pre)
        return self.evac(bank, eng,
                         lambda e, b=bank, d=dst, W=W, P=P, sc=scale:
                         e.tensor_scalar_mul(d, b.ap[0:W, 0:P], sc), pre=dst_pre)

    def build(self):
        nc = self.nc
        dr = {}
        add = lambda n, s, dt: dr.__setitem__(n, self.dram(n, s, dt))
        add('x', [T, D], F32)
        add('WqT', [D, NH * DQW], F32)
        add('WkT', [D, NH * DQW], F32)
        add('WvT', [D, D], F32)
        add('WogT', [D, D], F32)
        add('WiT', [D, NH], F32)
        add('WfT', [D, NH], F32)
        add('bi', [1, NH], F32)
        add('bf', [1, NH], F32)
        add('gam', [1, D], F32)
        add('WoutT', [D, D], F32)
        add('QgT', [D, FFI], U8)
        add('QuT', [D, FFI], U8)
        add('QdT', [FFI, D], U8)
        add('gsc', [128, FFI // 128], F32)
        add('usc', [128, FFI // 128], F32)
        add('dscr', [1, D], F32)
        add('wa', [D, D], F32)
        add('wi', [D, D], F32)
        add('wv', [D, D], F32)
        add('opw', [D, D], F32)
        add('wguI', [D, 2 * FFI], F32)
        add('wdn', [FFI, D], F32)
        add('dbias', [128, KT], F32)
        add('pnw', [1, D], F32)
        add('fnw', [1, D], F32)
        add('pnwT', [128, KT], F32)
        add('fnwT', [128, KT], F32)
        add('W1T', [D, FFI], F32)
        add('W3T', [D, FFI], F32)
        add('W2Tp', [FFI, D], F32)
        add('ident', [128, 128], BF16)
        add('H128', [128, 128], BF16)
        add('I2H64', [128, 128], BF16)
        add('triu01', [128, 128], F32)
        add('mask0_128', [NH, T], F32)
        add('maskneg_128', [NH, T], F32)
        add('mask0_64', [128, T], F32)
        add('mask1_64', [128, T], F32)
        add('fsel', [NH, 3], F32)
        add('fneg', [NH, 3], F32)
        add('fsel128', [128, 3], F32)
        add('fselb128', [128, 3], F32)
        add('ident_f', [128, 128], F32)
        self.dr = dr

        self.out = self.dram("out", [T, D], F32, kind="ExternalOutput")
        self.cc0_in = self.dram("cc0_in", [NH * 128, 260], F32, kind="Internal")
        self.cc0_out = self.dram("cc0_out", [4 * NH * 128, 260], F32, kind="Internal")
        self.cc1_in = self.dram("cc1_in", [128, 32], F32, kind="Internal")
        self.cc1_out = self.dram("cc1_out", [4 * 128, 32], F32, kind="Internal")
        self.xn1d = self.dram("xn1d", [T, D], BF16, kind="Internal")
        self.w2b = self.dram("w2b", [FFI, D], BF16, kind="Internal")
        self.recd = self.dram("recd", [D, T], F32, kind="Internal")
        self.cdgd = self.dram("cdgd", [D, T], F32, kind="Internal")

        POOL = 194560
        self.pool = self._enter(nc.sbuf_tensor("pool", [128, POOL], mybir.dt.int8))
        R = self.region
        R('CONST', 0, 8192)
        R('A', 8192, 32768)
        R('B', 40960, 32768)
        R('C', 73728, 32768)
        R('RES', 106496, 65536)
        R('SCAF', 106496, 12288)
        R('F', 118784, 12288)
        R('CAUG', 131072, 8192)
        R('DQ', 139264, 16384)
        R('KK', 155648, 16384)
        R('WBUF', 172032, 12288)
        R('REC', 106496, 32768)
        R('CDG', 139264, 32768)
        R('SCR', 184320, 8192)
        R('SMALL', 192512, 2048)

        v = {}
        cv = lambda dt, shape, off: self.view('CONST', dt, shape, byteoff=off)
        v['ident'] = cv(BF16, [128, 128], 0)
        v['H128'] = cv(BF16, [128, 128], 256)
        v['I2H64'] = cv(BF16, [128, 128], 512)
        v['triu01'] = cv(F32, [128, 128], 768)
        v['ones_b_col'] = cv(BF16, [128, 1], 1280)
        v['ones_f_col'] = cv(F32, [128, 1], 1284)
        v['zeros_col'] = cv(F32, [128, 1], 1288)
        v['neg_col'] = cv(F32, [128, 1], 1292)
        v['lneps_col'] = cv(F32, [128, 1], 4496)
        v['eps_col'] = cv(F32, [128, 1], 4500)
        v['ones_f_row'] = cv(F32, [1, 512], 1296)
        v['ones_b_row'] = cv(BF16, [1, 512], 3344)
        v['ones_f_all'] = cv(F32, [128, 128], 5888)
        v['ones_b_all'] = cv(BF16, [128, 128], 6400)
        v['bi'] = cv(F32, [1, NH], 4368)
        v['bf'] = cv(F32, [1, NH], 4400)
        v['fsel'] = cv(F32, [NH, 3], 4432)
        v['fneg'] = cv(F32, [NH, 3], 4448)
        v['fsel128'] = cv(F32, [128, 3], 4464)
        v['fselb128'] = cv(F32, [128, 3], 4480)
        v['gsc'] = cv(F32, [128, 64], 4608)
        v['usc'] = cv(F32, [128, 64], 4864)
        v['dbias'] = cv(F32, [128, KT], 5120)
        v['ident_f'] = cv(F32, [128, 128], 5376)
        v['pnwT'] = cv(F32, [128, KT], 6656)
        v['fnwT'] = cv(F32, [128, KT], 6720)
        self.v = v

        self.xnT = self.tiles('A', BF16, KT, T)
        self.vsb = self.tiles('B', BF16, NCH, D)
        self.ogg = self.tiles('C', BF16, NCH, D)
        self.x1 = self.tiles('RES', F32, NCH, D)
        self.qT = self.tiles('DQ', BF16, NH, T)
        self.ksb = self.tiles('KK', BF16, NCH, NH * DQW)

        self.scaf = {nm: self.view('SCAF', F32, [NH, T], byteoff=i * 4096)
                     for i, nm in enumerate(('csum', 'u', 'g'))}
        fv = lambda off: self.view('F', F32, [128, 64], byteoff=off)
        self.wcolsA = fv(0)
        self.emcolsA = fv(256)
        self.gcolsA = fv(512)
        self.negA = fv(768)
        self.facA = fv(1024)
        self.caug = self.view('CAUG', F32, [128, NH * DVW])

        sm = lambda dt, shape, off: self.view('SMALL', dt, shape, byteoff=off)
        self.mchain = sm(F32, [NH, 9], 0)
        self.lasts = sm(F32, [NH, 8], 64)
        self.ulasts = sm(F32, [NH, 8], 128)
        self.mlocct = sm(F32, [NH, 2], 192)
        self.mrun = sm(F32, [NH, 1], 256)
        self.tiny = [sm(F32, [NH, 1], 320 + 64 * i) for i in range(4)]
        self.ncols = sm(F32, [128, 8], 576)
        self.sumh_cols = self.ncols
        self.mu_col = sm(F32, [128, 1], 640)
        self.rstd_col = sm(F32, [128, 1], 704)
        self.s2_col = sm(F32, [128, 1], 768)
        self.t_col = sm(F32, [128, 1], 832)
        self.t2_col = sm(F32, [128, 1], 896)
        self.t3_col = sm(F32, [128, 1], 960)
        self.qn_col = sm(F32, [128, 1], 1024)
        self.den_col = sm(F32, [128, 1], 1088)
        self.facL_cols = sm(F32, [128, 4], 1120)
        self.Pcols = sm(F32, [128, 16], 1152)
        self.DLpack = sm(F32, [128, 32], 1280)
        self.state_col = sm(F32, [128, 1], 1472)
        self.nr_cols = sm(F32, [128, 8], 1536)
        self.fold2 = sm(F32, [1, 16], 1856)

        self.scr_x = self.view('SCR', F32, [128, 2048])

        for s in ("SL", "GL", "TE", "VE", "SE", "GP", "CC"):
            self.sems[s] = self._enter(nc.semaphore(s))
        self.block = self._enter(nc.Block())
        self.E = {n: Eng(self, n, s) for n, s in
                  (("sync", "SL"), ("gpsimd", "GL"), ("tensor", "TE"),
                   ("vector", "VE"), ("scalar", "SE"))}
        self.banks = [Bank(self._enter(nc.psum_tensor(f"ps{i}", [128, 512], F32)))
                      for i in range(3)]
        pt4 = self._enter(nc.psum_tensor("pt4", [128, 4096], BF16))
        self.trb = [Bank(pt4[:, i * 1024:(i + 1) * 1024]) for i in range(4)]
        self._trq = 0
        self.pssb = Bank(self._enter(nc.psum_tensor("pss", [128, 512], F32)))

        ph = self.cfg['phases']
        self.ph_consts()
        if 'noln' not in ph:
            self.ph_ln()
        else:
            self.ev_ln = self.evt("VE")
        if 'kvif' in ph: self.ph_kvif()
        if 'scaffold' in ph: self.ph_scaffold(1)
        if 'summary' in ph: self.ph_summary()
        if 'qog' in ph: self.ph_qog()
        if 'fold' in ph:
            self.ph_fold()
            self.ph_scaffold(2)
        if 'scan2' in ph: self.ph_scan2()
        if 'wout' in ph: self.ph_wout()
        if 'ffn0' in ph: self.ph_ffn0()
        if 'l1pre' in ph: self.ph_l1pre()
        if 'w2prep' in ph: self.ph_w2prep()
        if 'l1attn' in ph: self.ph_l1attn()
        if 'l1post' in ph: self.ph_l1post()
        if 'l1ffn' in ph: self.ph_l1ffn()
        self.ph_dbg_stores()
        self.ph_final_wait()

        self.block.sync(lambda e: [f(e) for f in self.E['sync'].q])
        self.block.gpsimd(lambda e: [f(e) for f in self.E['gpsimd'].q])
        self.block.tensor(lambda e: [f(e) for f in self.E['tensor'].q])
        self.block.vector(lambda e: [f(e) for f in self.E['vector'].q])
        self.block.scalar(lambda e: [f(e) for f in self.E['scalar'].q])
        return self.nc

    def ph_consts(self):
        dr, v = self.dr, self.v
        sy, gp = self.E['sync'], self.E['gpsimd']
        for name, dst in (("ident", v['ident']), ("H128", v['H128']),
                          ("I2H64", v['I2H64']), ("triu01", v['triu01']),
                          ("bi", v['bi']), ("bf", v['bf']),
                          ("fsel", v['fsel']), ("fneg", v['fneg']),
                          ("fsel128", v['fsel128']), ("fselb128", v['fselb128']),
                          ("gsc", v['gsc']), ("usc", v['usc']),
                          ("pnwT", v['pnwT']), ("fnwT", v['fnwT']),
                          ("dbias", v['dbias']), ("ident_f", v['ident_f'])):
            sy.op(lambda e, d=dst, n=name: e.dma_start(out=d, in_=dr[n].ap()),
                  inc=16)
        for dst, val in ((v['ones_b_col'], 1.0), (v['ones_f_col'], 1.0),
                         (v['zeros_col'], 0.0), (v['neg_col'], NEG),
                         (v['lneps_col'], LNEPS), (v['eps_col'], EPS),
                         (v['ones_f_row'], 1.0), (v['ones_b_row'], 1.0),
                         (v['ones_f_all'], 1.0), (v['ones_b_all'], 1.0)):
            gp.op(lambda e, d=dst, vv=val: e.memset(d, vv), sem="GP")
        self.ev_consts = [self.evt("SL"), self.evt("GP")]

    def ph_ln(self):
        dr = self.dr
        xt = dr['x'].ap().rearrange("(n p) d -> n p d", p=128)
        sq_dump = self.view('WBUF', F32, [128, 1024], byteoff=0)
        xn_scr = self.view('WBUF', BF16, [128, 2048], byteoff=4096)
        scr_x = self.view('SCR', F32, [128, 2048])
        sy, ve, sc, te = (self.E[k] for k in ('sync', 'vector', 'scalar', 'tensor'))
        ev_xread = None
        ln_tiles = self.cfg.get('ln_tiles', NCH)
        ln_tr = self.cfg.get('ln_tr', True)
        for i in range(ln_tiles):
            sy.wait(ev_xread)
            ev_ld = sy.op(lambda e, i=i: e.dma_start(out=scr_x, in_=xt[i]), inc=16)
            ve.wait(ev_ld)
            ve.opb(lambda e: e.tensor_reduce(self.t_col, scr_x, AX, ALU.add))
            ve.opb(lambda e: e.tensor_scalar_mul(self.mu_col, self.t_col, 1.0 / D))
            if self.cfg.get('ln_stage') == 1:
                self.dbg('tcol', self.t_col, [self.evt("VE")])
                self.dbg('mu', self.mu_col, [self.evt("VE")])
                self.ev_ln = self.evt("VE")
                return
            sc.wait(ev_ld)
            sc.raw(lambda e: e.activation(sq_dump, scr_x[:, 0:1024], AF.Square,
                                          accum_out=self.s2_col))
            ev_s2 = sc.op(lambda e: e.activation(sq_dump, scr_x[:, 1024:2048],
                                                 AF.Square, accum_out=self.t2_col))
            ve.wait(ev_s2)
            ve.opb(lambda e: e.tensor_add(self.s2_col, self.s2_col, self.t2_col))
            ve.opb(lambda e: e.tensor_tensor(self.t2_col, self.mu_col,
                                             self.mu_col, ALU.mult))
            ev_var = ve.opb(lambda e: e.scalar_tensor_tensor(
                self.t_col, self.s2_col, 1.0 / D, self.t2_col,
                ALU.mult, ALU.subtract))
            if self.cfg.get('ln_stage') == 2:
                self.dbg('mu', self.mu_col, [ev_var, self.evt("SE")])
                self.dbg('s2', self.s2_col, [ev_var])
                self.dbg('var', self.t_col, [ev_var])
                self.ev_ln = self.evt("VE")
                return
            sc.wait(ev_var)
            ev_sd = sc.op(lambda e: e.activation(self.t2_col, self.t_col,
                                                 AF.Sqrt, bias=self.v['lneps_col']))
            ve.wait(ev_sd)
            ve.opb(lambda e: e.reciprocal(self.rstd_col, self.t2_col))
            ev_xn = ve.op(lambda e: e.tensor_scalar(
                xn_scr, scr_x, self.mu_col, self.rstd_col,
                ALU.subtract, ALU.mult))
            if self.cfg.get('ln_stage') == 3:
                self.dbg('mu', self.mu_col, [ev_xn])
                self.dbg('rstd', self.rstd_col, [ev_xn])
                self.dbg('xn7', xn_scr, [ev_xn])
                self.ev_ln = self.evt("VE")
                return
            ev_xread = ev_xn
            if ln_tr:
                for k in range(KT):
                    self.tr_one(xn_scr[:, k * 128:(k + 1) * 128],
                                self.xnT[k][:, i * 128:(i + 1) * 128],
                                pre=[ev_xn])
        self.ev_ln = self.evt("VE")
        self.dbg('mu', self.mu_col, [self.ev_ln])
        self.dbg('rstd', self.rstd_col, [self.ev_ln])
        self.dbg('s2', self.s2_col, [self.ev_ln])
        self.dbg('xn7', xn_scr, [self.ev_ln])
        self.dbg('xnT0', self.xnT[0], [self.ev_ln])
        self.dbg('xnT15', self.xnT[15], [self.ev_ln])

    def sumx_compute(self, dest_row):
        te = self.E['tensor']
        te.wait(self.ev_ln)
        te.wait(self.ev_consts[1])
        bank = self.pssb
        mms = []
        for half in range(2):
            for k in range(KT):
                mms.append((bank.ap[0:1, half * 512:(half + 1) * 512],
                            self.v['ones_b_col'],
                            self.xnT[k][:, half * 512:(half + 1) * 512]))
        te.wait(bank.r_ev)
        for idx, (osl, lh, rh) in enumerate(mms[:-1]):
            k = idx % KT
            te.raw(lambda e, osl=osl, lh=lh, rh=rh, s=(k == 0), p=(k == KT - 1):
                   e.matmul(osl, lh, rh, start=s, stop=p))
        osl, lh, rh = mms[-1]
        bank.w_ev = te.op(lambda e, osl=osl, lh=lh, rh=rh:
                          e.matmul(osl, lh, rh, start=False, stop=True))
        return self.evac(bank, 'vector',
                         lambda e: e.tensor_copy(dest_row, bank.ap[0:1, 0:1024]))

    def stream_proj(self, wdram, writer, n_out, osl_w, bufs, extra_mm=None,
                    lhs_T=None, init_free=(None, None)):
        gp = self.E['gpsimd']
        lhs = lhs_T or self.xnT
        ev_free = list(init_free)
        for si in range(n_out // osl_w):
            o0 = si * osl_w
            b = si % 2
            gp.wait(ev_free[b])
            src = wdram.ap()[:, o0:o0 + osl_w].rearrange("(n p) o -> p n o", p=128)
            d0 = bufs[b][0]
            dst3 = rep(d0, [d0.ap[0], [osl_w, KT], [1, osl_w]])
            ev_w = gp.op(lambda e, d=dst3, s=src: e.dma_start(out=d, in_=s), inc=16)
            for ts in range(NCH):
                bank = self.banks[(si * NCH + ts) % 3]
                mms = [(bank.ap[:, 0:osl_w],
                        lhs[k][:, ts * 128:(ts + 1) * 128], bufs[b][k])
                       for k in range(KT)]
                if extra_mm is not None:
                    mms += extra_mm(o0, ts)
                self.mm_group(bank, mms, pre=[ev_w, self.ev_ln])
                writer(o0, ts, bank)
            ev_free[b] = self.evt("TE")

    def ph_kvif(self):
        dr = self.dr
        skip_kv = self.cfg.get('skip_kv', False)
        skip_if = self.cfg.get('skip_if', False)
        bufA = self.tiles('WBUF', BF16, KT, 256, byteoff=0)
        bufB = self.tiles('SCR', BF16, KT, 256, byteoff=0)
        if skip_kv:
            self.ev_kv = self.evt("VE")
        else:
            self._kv_body(dr, bufA, bufB)
        if skip_if:
            self.ev_if = self.evt("VE")
            return
        self._if_body(dr)

    def _kv_body(self, dr, bufA, bufB):

        def wr_k(o0, ts, bank):
            self.evac(bank, 'vector', lambda e, o0=o0, ts=ts, bank=bank:
                      e.tensor_copy(self.ksb[ts][:, o0:o0 + 256],
                                    bank.ap[:, 0:256]))
        self.stream_proj(dr['WkT'], wr_k, NH * DQW, 256, (bufA, bufB),
                         init_free=(self.ev_ln, self.ev_ln))

        def wr_v(o0, ts, bank):
            self.evac(bank, 'vector', lambda e, o0=o0, ts=ts, bank=bank:
                      e.tensor_copy(self.vsb[ts][:, o0:o0 + 256],
                                    bank.ap[:, 0:256]))
        ev_t = self.evt("TE")
        self.stream_proj(dr['WvT'], wr_v, D, 256, (bufA, bufB),
                         init_free=(ev_t, ev_t))
        self.ev_kv = self.evt("VE")

    def _if_body(self, dr):
        te, ve, sc, gp = (self.E[k] for k in ('tensor', 'vector', 'scalar', 'gpsimd'))
        wif = self.tiles('WBUF', BF16, KT, 16, byteoff=0)
        gp.wait(self.evt("TE"))
        d0 = wif[0]
        dst_i = rep(d0, [d0.ap[0], [16, KT], [1, NH]])
        dst_f = rep(d0, [d0.ap[0], [16, KT], [1, NH]], doffel=NH)
        src_i = dr['WiT'].ap().rearrange("(n p) o -> p n o", p=128)
        src_f = dr['WfT'].ap().rearrange("(n p) o -> p n o", p=128)
        gp.op(lambda e: e.dma_start(out=dst_i, in_=src_i), inc=16)
        ev_wf = gp.op(lambda e: e.dma_start(out=dst_f, in_=src_f), inc=16)
        tanh_scr = self.view('WBUF', F32, [NH, 512], byteoff=8192)
        for part in range(2):
            outv = self.iT_view() if part == 0 else self.fT_view()
            brow = self.v['bi'] if part == 0 else self.v['bf']
            for th in range(2):
                bank = self.pssb
                mms = [(bank.ap[0:NH, 0:512],
                        wif[k][:, part * NH:part * NH + NH],
                        self.xnT[k][:, th * 512:(th + 1) * 512])
                       for k in range(KT)]
                mms.append((bank.ap[0:NH, 0:512], brow, self.v['ones_f_row']))
                self.mm_group(bank, mms, pre=[ev_wf, self.ev_ln,
                                              self.ev_consts[0], self.ev_consts[1]])
                ev_t = self.evac(bank, 'scalar',
                                 lambda e, bank=bank: e.activation(
                                     tanh_scr, bank.ap[0:NH, 0:512],
                                     AF.Tanh, scale=1.0 / 15.0),
                                 pre=[self.evt("VE")])
                ve.wait(ev_t)
                ve.op(lambda e, outv=outv, th=th: e.tensor_scalar_mul(
                    outv[:, th * 512:(th + 1) * 512], tanh_scr, 15.0))
                sc.wait(self.evt("VE"))
        self.ev_if = self.evt("VE")
        self.dbg('k0', self.ksb[0], [self.ev_kv])
        self.dbg('v7', self.vsb[7], [self.ev_kv])
        self.dbg('iT', self.iT_view(), [self.ev_if])
        self.dbg('fT', self.fT_view(), [self.ev_if])

    def iT_view(self):
        return self.view('SCR', F32, [NH, T], byteoff=0)

    def fT_view(self):
        return self.view('SCR', F32, [NH, T], byteoff=4096)

    def scafT(self, i):
        return self.view('WBUF', F32, [NH, T], byteoff=i * 4096)

    def fbase(self, h):
        return 32 * (h % 4), (h // 4)

    def ph_scaffold(self, stage):
        ve, sc, gp, sy = (self.E[k] for k in ('vector', 'scalar', 'gpsimd', 'sync'))
        t0, t1, t2 = self.scafT(0), self.scafT(1), self.scafT(2)
        csum, u, g = self.scaf['csum'], self.scaf['u'], self.scaf['g']
        iT, fT = self.iT_view(), self.fT_view()
        if stage == 1:
            ve.wait(self.evt("TE"))
            ve.wait(self.ev_if)
            ve.opb(lambda e: e.scalar_tensor_tensor(t0, fT, -1.0, fT,
                                                    ALU.mult, ALU.max))
            sc.wait(self.evt("VE"))
            ev_exp = sc.opb(lambda e: e.activation(t1, t0, AF.Exp, scale=-1.0))
            ev_l = sc.opb(lambda e: e.activation(
                t1, t1, AF.Ln, bias=self.v['ones_f_col'][0:NH, :]))
            ve.wait(ev_exp)
            ve.opb(lambda e: e.tensor_scalar_min(t0, fT, 0.0))
            ve.wait(ev_l)
            ve.opb(lambda e: e.tensor_tensor(t1, t0, t1, ALU.subtract))
            sy.waits([self.evt("TE"), self.evt("VE"), self.evt("SE")])
            ev_m0 = sy.op(lambda e: e.dma_start(out=t0, in_=self.dr['mask0_128'].ap()),
                          inc=16)
            ve.wait(ev_m0)
            ve.opb(lambda e: e.tensor_tensor_scan(csum, t0, t1, 0.0,
                                                  ALU.mult, ALU.add))
            sy.wait(self.evt("VE"))
            ev_mn = sy.op(lambda e: e.dma_start(out=t0, in_=self.dr['maskneg_128'].ap()),
                          inc=16)
            ve.opb(lambda e: e.tensor_tensor(g, iT, csum, ALU.subtract))
            ve.wait(ev_mn)
            ve.opb(lambda e: e.tensor_tensor_scan(u, t0, g, NEG, ALU.add, ALU.max))
            ve.opb(lambda e: e.tensor_copy(self.lasts, csum[:, TC - 1::TC]))
            ve.opb(lambda e: e.tensor_copy(self.ulasts, u[:, TC - 1::TC]))
            ve.opb(lambda e: e.memset(self.mchain[:, 0:1], NEG))
        else:
            ve.wait(self.evt("TE"))
            ve.opb(lambda e: e.tensor_copy(self.mchain[:, 0:1], self.mrun))
        for c in range(NCH):
            ve.opb(lambda e, c=c: e.tensor_tensor(
                self.tiny[0], self.mchain[:, c:c + 1],
                self.ulasts[:, c:c + 1], ALU.max))
            ve.opb(lambda e, c=c: e.tensor_tensor(
                self.mchain[:, c + 1:c + 2], self.lasts[:, c:c + 1],
                self.tiny[0], ALU.add))
        if stage == 1:
            ve.opb(lambda e: e.tensor_copy(self.mlocct[:, 0:1],
                                           self.mchain[:, NCH:NCH + 1]))
            ve.opb(lambda e: e.tensor_reduce(self.mlocct[:, 1:2], self.lasts,
                                             AX, ALU.add))
        def w3(ap):
            return ap.rearrange("p (a b) -> p a b", b=TC)
        mb = dataclasses.replace(self.mchain[:, 0:NCH],
                                 ap=[self.mchain.ap[0], [1, NCH], [0, TC]])
        ve.opb(lambda e: e.tensor_tensor(w3(t1), w3(u), mb, ALU.max))
        mxL = self.view('SMALL', F32, [NH, 8], byteoff=1792)
        ve.opb(lambda e, mxL=mxL: e.tensor_copy(mxL, t1[:, TC - 1::TC]))
        mxLb = dataclasses.replace(mxL, ap=[mxL.ap[0], [1, NCH], [0, TC]])
        ve.opb(lambda e, mxLb=mxLb: e.tensor_tensor(w3(t2), w3(g), mxLb,
                                                    ALU.subtract))
        ev_wpre = self.evt("VE")
        sc.wait(ev_wpre)
        ev_w = sc.opb(lambda e: e.activation(t2, t2, AF.Exp))
        for c in range(NCH):
            self.tr_one(t2[0:NH, c * TC:(c + 1) * TC],
                        self.wcolsA[:, c * NH:(c + 1) * NH],
                        pre=[ev_w], bank=self.pssb, ident=self.v['ident_f'])
        ve.opb(lambda e, mb=mb: e.tensor_tensor(w3(t0), mb, w3(t1), ALU.subtract))
        ev_fpre = self.evt("VE")
        sc.wait(ev_fpre)
        ev_fac = sc.opb(lambda e: e.activation(t0, t0, AF.Exp))
        for c in range(NCH):
            self.tr_one(t0[0:NH, c * TC:(c + 1) * TC],
                        self.facA[:, c * NH:(c + 1) * NH],
                        pre=[ev_fac], bank=self.pssb, ident=self.v['ident_f'])
        if stage == 2:
            ve.wait(self.pssb.r_ev)
            ve.opb(lambda e: e.tensor_tensor(t0, csum, t1, ALU.add))
            ev_mt = self.evt("VE")
            sc.wait(ev_mt)
            ev_em = sc.opb(lambda e: e.activation(t0, t0, AF.Exp, scale=-1.0))
            for c in range(NCH):
                self.tr_one(t0[0:NH, c * TC:(c + 1) * TC],
                            self.emcolsA[:, c * NH:(c + 1) * NH],
                            pre=[ev_em], bank=self.pssb, ident=self.v['ident_f'])
            for c in range(NCH):
                self.tr_one(g[0:NH, c * TC:(c + 1) * TC],
                            self.gcolsA[:, c * NH:(c + 1) * NH],
                            bank=self.pssb, ident=self.v['ident_f'])
            ve.wait(self.pssb.r_ev)
            ve.opb(lambda e: e.tensor_scalar_mul(t1, t1, -1.0))
            ev_neg = self.evt("VE")
            for c in range(NCH):
                self.tr_one(t1[0:NH, c * TC:(c + 1) * TC],
                            self.negA[:, c * NH:(c + 1) * NH],
                            pre=[ev_neg], bank=self.pssb, ident=self.v['ident_f'])
            self.ev_scaf2 = [self.evt("VE"), self.evt("SE"), self.evt("TE")]
        else:
            self.ev_scaf1 = [self.evt("VE"), self.evt("SE"), self.evt("TE")]
        self.dbg('csum%d' % stage, csum, [self.evt("VE")])
        self.dbg('u%d' % stage, u, [self.evt("VE")])
        self.dbg('g%d' % stage, g, [self.evt("VE")])
        self.dbg('wcols%d' % stage, self.wcolsA, [self.evt("VE")])
        self.dbg('mchain%d' % stage, self.mchain, [self.evt("VE")])

    def col_bcast_T(self, col, dst, width=128, pre=()):
        ve = self.E['vector']
        bt = self.view('SCR', F32, [128, 128], byteoff=5184 + 1024)
        ve.waits(pre)
        ve.opb(lambda e, col=col: e.tensor_scalar(
            bt[:, 0:width], self.v['ones_f_all'][:, 0:width], col, None,
            ALU.mult))
        return self.tr_one(bt[:, 0:width], dst, bank=self.pssb,
                           ident=self.v['ident_f'], pre=[self.evt("VE")])

    def state_update_loop(self, which):
        ve, te, gp = self.E['vector'], self.E['tensor'], self.E['gpsimd']
        kwscr = [self.view('SCR', BF16, [128, 128], byteoff=o) for o in (0, 256)]
        kw_ev = [None, None]
        for c in range(NCH):
            for h in range(NH):
                i = c * NH + h
                buf = i % 2
                ve.wait(kw_ev[buf])
                ve.waits([self.ev_kv, self.evt("TE") if False else None])
                ve.opb(lambda e, c=c, h=h, buf=buf: e.tensor_scalar(
                    kwscr[buf], self.ksb[c][:, h * DQW:(h + 1) * DQW],
                    self.wcolsA[:, c * NH + h:c * NH + h + 1], None, ALU.mult))
                bank = self.banks[i % 3]
                mms = [(bank.ap[:, 0:256], kwscr[buf],
                        self.vsb[c][:, h * DVW:(h + 1) * DVW]),
                       (bank.ap[:, 256:257], kwscr[buf], self.v['ones_b_col'])]
                self.mm_group(bank, mms, pre=[self.evt("VE")])
                kw_ev[buf] = bank.w_ev
                fbt = self.view('SCR', F32, [128, 128], byteoff=1024)
                self.col_bcast_T(self.facA[:, c * NH + h:c * NH + h + 1], fbt)
                ve.opb(lambda e, fbt=fbt: e.tensor_copy(self.facL_cols[:, 0:1],
                                                        fbt[:, 127:128]))
                ve.wait(bank.w_ev)
                ve.opb(lambda e, c=c, h=h, bank=bank: e.scalar_tensor_tensor(
                    self.caug[:, h * DVW:(h + 1) * DVW],
                    self.caug[:, h * DVW:(h + 1) * DVW],
                    self.facL_cols[:, 0:1], bank.ap[:, 0:256],
                    ALU.mult, ALU.add))
                ev = ve.opb(lambda e, h=h, bank=bank: e.scalar_tensor_tensor(
                    self.ncols[:, h:h + 1], self.ncols[:, h:h + 1],
                    self.facL_cols[:, 0:1], bank.ap[:, 256:257],
                    ALU.mult, ALU.add))
                bank.r_ev = ev

    def ph_summary(self):
        ve, te, gp = self.E['vector'], self.E['tensor'], self.E['gpsimd']
        gp.waits(self.ev_scaf1)
        gp.op(lambda e: e.memset(self.caug, 0.0), sem="GP")
        ev_ms = gp.op(lambda e: e.memset(self.ncols, 0.0), sem="GP")
        ve.wait(ev_ms)
        self.state_update_loop(1)
        ev_v = self.evt("VE")
        gp.wait(ev_v)
        co = self.cc0_out
        ci = self.cc0_in.ap().rearrange("(h p) c -> p h c", p=128)
        gp.op(lambda e, ci=ci: e.dma_start(out=ci[:, :, 0:256],
                                           in_=self.caug.rearrange(
                                               "p (h c) -> p h c", c=256)), inc=16)
        gp.op(lambda e, ci=ci: e.dma_start(
            out=ci[:, :, 256:257],
            in_=dataclasses.replace(self.ncols,
                                    ap=[self.ncols.ap[0], [1, NH], [1, 1]]),
            allow_slow_non_contiguous=True), inc=16)
        base = self.cc0_in.ap()
        mct = dataclasses.replace(base, ap=[[260 * 128, NH], [260, 128], [1, 2]],
                                  offset=257)
        mci = dataclasses.replace(self.mlocct,
                                  ap=[self.mlocct.ap[0], [0, 128], [1, 2]])
        gp.op(lambda e, mct=mct, mci=mci: e.dma_start(out=mct, in_=mci), inc=16)
        pad = dataclasses.replace(base, ap=[[260 * 128, NH], [260, 128], [1, 1]],
                                  offset=259)
        padi = dataclasses.replace(self.mlocct,
                                   ap=[self.mlocct.ap[0], [0, 128], [1, 1]])
        gp.op(lambda e, pad=pad, padi=padi: e.dma_start(
            out=pad, in_=padi, allow_slow_non_contiguous=True), inc=16)
        ev_sum = self.evt("GL")
        gp.wait(ev_sum)
        self.ev_cc0 = gp.op(lambda e: e.collective_compute(
            "AllGather", ALU.bypass,
            replica_groups=[[0, 1, 2, 3], [4, 5, 6, 7]],
            ins=[self.cc0_in.ap()], outs=[self.cc0_out.ap()]), inc=1, sem="CC")
        self.dbg('caug1', self.caug, [self.evt("VE")])
        self.dbg('ncols1', self.ncols, [self.evt("VE")])

    def ph_qog(self):
        dr = self.dr
        ve, te, sc, gp = (self.E[k] for k in ('vector', 'tensor', 'scalar',
                                              'gpsimd'))
        gp.wait(self.evt("CC"))
        wq = [self.view('F', BF16, [128, 128], byteoff=b * 2048)
              for b in range(2)]
        gamb = self.view('SCR', BF16, [128, 2048], byteoff=0)
        gamr = self.view('SCR', BF16, [1, 2048], byteoff=4096)
        gp.waits([self.evt("VE"), self.evt("SE"), self.evt("TE")])
        ev_gr = gp.op(lambda e: e.dma_start(out=gamr, in_=dr['gam'].ap()), inc=16)
        for j in range(4):
            bank = self.banks[j % 3]
            self.mm_group(bank, [(bank.ap[:, 0:512],
                                  self.v['ones_b_row'][0:1, 0:128],
                                  gamr[0:1, j * 512:(j + 1) * 512])],
                          pre=[ev_gr, self.ev_consts[1]])
            self.evac(bank, 'vector',
                      lambda e, j=j, bank=bank: e.tensor_copy(
                          gamb[:, j * 512:(j + 1) * 512], bank.ap[:, 0:512]))
        ev_gamb = self.evt("VE")

        wqb = [self.tiles('F', BF16, KT, 128, byteoff=0),
               self.tiles('F', BF16, KT, 128, byteoff=4096)]
        ev0 = [self.evt("TE"), self.evt("VE"), self.evt("SE")]
        ev_free = [ev0[0], ev0[0]]
        gp.waits(ev0)
        for sl in range(NH):
            b = sl % 2
            gp.wait(ev_free[b])
            src = dr['WqT'].ap()[:, sl * 128:(sl + 1) * 128].rearrange(
                "(n p) o -> p n o", p=128)
            d0 = wqb[b][0]
            dst3 = rep(d0, [d0.ap[0], [128, KT], [1, 128]])
            ev_w = gp.op(lambda e, d=dst3, s=src: e.dma_start(out=d, in_=s),
                         inc=16)
            for th in range(2):
                bank = self.banks[(sl * 2 + th) % 3]
                mms = [(bank.ap[:, 0:512], wqb[b][k],
                        self.xnT[k][:, th * 512:(th + 1) * 512])
                       for k in range(KT)]
                self.mm_group(bank, mms, pre=[ev_w, self.ev_ln])
                self.evac(bank, 'vector',
                          lambda e, sl=sl, th=th, bank=bank: e.tensor_scalar_mul(
                              self.qT[sl][:, th * 512:(th + 1) * 512],
                              bank.ap[:, 0:512], SCALE_Q))
            ev_free[b] = self.evt("TE")
        self.ev_q = self.evt("VE")
        if 'qT3early' in self.cfg.get('dbg', ()):
            t = self.dram("dbg_qT3early", [128, T], BF16, kind="ExternalOutput")
            sy = self.E['sync']
            sy.wait(self.ev_q)
            sy.op(lambda e, t=t: e.dma_start(out=t.ap(), in_=self.qT[3]), inc=16)

        sgscr = self.view('SCR', BF16, [128, 256], byteoff=4096)
        ogb = [self.tiles('F', BF16, KT, 256, byteoff=0),
               self.tiles('WBUF', BF16, KT, 256, byteoff=0)]
        gp.waits([self.evt("VE"), self.evt("SE")])
        ev_free = [self.evt("TE"), self.evt("TE")]
        for si in range(NH):
            o0 = si * 256
            b = si % 2
            gp.wait(ev_free[b])
            src = dr['WogT'].ap()[:, o0:o0 + 256].rearrange("(n p) o -> p n o",
                                                            p=128)
            d0 = ogb[b][0]
            dst3 = rep(d0, [d0.ap[0], [256, KT], [1, 256]])
            ev_w = gp.op(lambda e, d=dst3, s=src: e.dma_start(out=d, in_=s),
                         inc=16)
            for ts in range(NCH):
                bank = self.banks[(si * NCH + ts) % 3]
                mms = [(bank.ap[:, 0:256],
                        self.xnT[k][:, ts * 128:(ts + 1) * 128], ogb[b][k])
                       for k in range(KT)]
                self.mm_group(bank, mms, pre=[ev_w, self.ev_ln])
                ev_sg = self.evac(bank, 'scalar',
                                  lambda e, bank=bank: e.activation(
                                      sgscr, bank.ap[:, 0:256], AF.Sigmoid),
                                  pre=[self.evt("VE")])
                ve.wait(ev_sg)
                ve.wait(ev_gamb)
                ve.op(lambda e, ts=ts, o0=o0: e.tensor_tensor(
                    self.ogg[ts][:, o0:o0 + 256], sgscr,
                    gamb[:, o0:o0 + 256], ALU.mult))
            ev_free[b] = self.evt("TE")
        self.ev_og = self.evt("VE")
        self.dbg('qT0', self.qT[0], [self.ev_q])
        self.dbg('ogg0', self.ogg[0], [self.ev_og])

    def ph_fold(self):
        ve, sc, gp, te = (self.E[k] for k in ('vector', 'scalar', 'gpsimd',
                                              'tensor'))
        gp.wait(self.evt("CC"))
        gp.op(lambda e: e.memset(self.caug, 0.0), sem="GP")
        gp.op(lambda e: e.memset(self.ncols, 0.0), sem="GP")
        ev_ms = self.evt("GP")
        ve.wait(ev_ms)
        ve.opb(lambda e: e.memset(self.mrun, 0.0))
        crs = self.view('SCR', F32, [128, 2048], byteoff=0)
        tmp = self.view('WBUF', F32, [128, 256], byteoff=0)
        gp.waits([self.evt("VE"), self.evt("SE"), self.evt("TE")])
        for r in range(3):
            gp.wait(self.evt("VE"))
            co = self.cc0_out.ap()
            base = r * NH * 128
            ev1 = gp.op(lambda e, base=base: e.dma_start(
                out=crs.rearrange("p (h c) -> p h c", c=256),
                in_=self.cc0_out.ap()[base:base + NH * 128, 0:256].rearrange(
                    "(h p) c -> p h c", p=128)), inc=16)
            ev2 = gp.op(lambda e, base=base: e.dma_start(
                out=dataclasses.replace(self.nr_cols,
                                        ap=[self.nr_cols.ap[0], [1, NH], [1, 1]]),
                in_=self.cc0_out.ap()[base:base + NH * 128, 256:257].rearrange(
                    "(h p) c -> p h c", p=128),
                allow_slow_non_contiguous=True), inc=16)
            mct = dataclasses.replace(co, ap=[[260 * 128, NH], [1, 2]],
                                      offset=base * 260 + 257)
            ev3 = gp.op(lambda e, mct=mct: e.dma_start(out=self.mlocct[:, 0:2],
                                                       in_=mct), inc=16)
            ve.waits([ev1, ev2, ev3])
            ve.opb(lambda e: e.tensor_tensor(self.tiny[0], self.mlocct[:, 0:1],
                                             self.mlocct[:, 1:2], ALU.subtract))
            ve.opb(lambda e, r=r: e.tensor_tensor(
                self.tiny[0], self.tiny[0], self.v['fsel'][:, r:r + 1],
                ALU.mult))
            ve.opb(lambda e, r=r: e.tensor_tensor(
                self.tiny[0], self.tiny[0], self.v['fneg'][:, r:r + 1],
                ALU.add))
            ve.opb(lambda e, r=r: e.tensor_tensor(
                self.tiny[1], self.mlocct[:, 1:2], self.v['fsel'][:, r:r + 1],
                ALU.mult))
            ve.opb(lambda e: e.tensor_tensor(self.tiny[2], self.mrun,
                                             self.tiny[0], ALU.max))
            ve.opb(lambda e: e.tensor_tensor(self.tiny[3], self.mrun,
                                             self.tiny[2], ALU.subtract))
            sc.wait(self.evt("VE"))
            ev_fr = sc.opb(lambda e: e.activation(self.tiny[3], self.tiny[3],
                                                  AF.Exp))
            ve.opb(lambda e: e.tensor_tensor(self.tiny[0], self.tiny[0],
                                             self.tiny[2], ALU.subtract))
            sc.wait(self.evt("VE"))
            ev_fn = sc.opb(lambda e: e.activation(self.tiny[0], self.tiny[0],
                                                  AF.Exp))
            ve.waits([ev_fr, ev_fn])
            ve.opb(lambda e: e.tensor_tensor(self.mrun, self.tiny[1],
                                             self.tiny[2], ALU.add))
            self.tr_one(self.tiny[3], self.fold2[0:1, 0:8], pre=[ev_fr],
                        bank=self.pssb, ident=self.v['ident_f'])
            self.tr_one(self.tiny[0], self.fold2[0:1, 8:16], pre=[ev_fn],
                        bank=self.pssb, ident=self.v['ident_f'])
            for h in range(NH):
                te.wait(self.pssb.r_ev)
                te.wait(self.evt("VE"))
                self.pssb.w_ev = te.op(lambda e, h=h: e.matmul(
                    self.pssb.ap[:, 300:301],
                    self.v['ones_f_all'][0:1, 0:128],
                    self.fold2[0:1, h:h + 1], start=True, stop=False))
                self.pssb.w_ev = te.op(lambda e, h=h: e.matmul(
                    self.pssb.ap[:, 301:302],
                    self.v['ones_f_all'][0:1, 0:128],
                    self.fold2[0:1, NH + h:NH + h + 1],
                    start=False, stop=True))
                self.evac(self.pssb, 'vector',
                          lambda e: e.tensor_copy(self.facL_cols[:, 0:2],
                                                  self.pssb.ap[:, 300:302]))
                ve.wait(self.evt("VE"))
                ve.opb(lambda e, h=h: e.tensor_scalar(
                    tmp, crs[:, h * 256:(h + 1) * 256],
                    self.facL_cols[:, 1:2], None, ALU.mult))
                ve.opb(lambda e, h=h: e.scalar_tensor_tensor(
                    self.caug[:, h * 256:(h + 1) * 256],
                    self.caug[:, h * 256:(h + 1) * 256],
                    self.facL_cols[:, 0:1], tmp, ALU.mult, ALU.add))
                ve.opb(lambda e, h=h: e.tensor_scalar(
                    self.facL_cols[:, 2:3], self.nr_cols[:, h:h + 1],
                    self.facL_cols[:, 1:2], None, ALU.mult))
                ve.opb(lambda e, h=h: e.scalar_tensor_tensor(
                    self.ncols[:, h:h + 1], self.ncols[:, h:h + 1],
                    self.facL_cols[:, 0:1], self.facL_cols[:, 2:3],
                    ALU.mult, ALU.add))
        self.dbg('caugF', self.caug, [self.evt("VE")])
        self.dbg('mrunF', self.mrun, [self.evt("VE")])
        if 'ccraw' in self.cfg.get('dbg', ()):
            t = self.dram("dbg_ccraw", [4 * NH * 128, 260], F32,
                          kind="ExternalOutput")
            sy2 = self.E['sync']
            sy2.wait(self.evt("CC"))
            sy2.op(lambda e, t=t: e.dma_start(out=t.ap(), in_=self.cc0_out.ap()),
                   inc=16)

    def ph_scan2(self):
        ve, sc, te, gp = (self.E[k] for k in ('vector', 'scalar', 'tensor',
                                              'gpsimd'))
        S = lambda dt, shape, off: self.view('SCR', dt, shape, byteoff=off)
        kbscr = [S(BF16, [128, 128], 0), S(BF16, [128, 128], 256)]
        kwscr = [S(BF16, [128, 128], 512), S(BF16, [128, 128], 768)]
        sqscr = S(F32, [128, 128], 1024)
        dscr = S(F32, [128, 128], 1536)
        sdscr = S(BF16, [128, 128], 2048)
        qfscr = S(BF16, [128, 128], 2304)
        cbf = S(BF16, [128, 257], 2560)
        hsc = S(F32, [128, 256], 3136)
        sqd = S(F32, [128, 256], 4160)
        t256 = S(F32, [128, 256], 5184)
        m2scr = S(F32, [128, 128], 6272)
        fbscr = S(F32, [128, 128], 6784)
        kb_ev = [None, None]
        kw_ev = [None, None]
        for c in range(NCH):
            for h in range(NH):
                i = c * NH + h
                buf = i % 2
                ci = c * NH + h
                ve.waits([self.evt("SE"), self.evt("TE")])
                sc.waits([self.evt("VE"), self.evt("TE")])
                te.waits([self.evt("VE"), self.evt("SE")])
                ev_kb = self.tr_one(self.ksb[c][:, h * DQW:(h + 1) * DQW],
                                    kbscr[buf], pre=[kb_ev[buf] or None],
                                    dst_pre=[kb_ev[buf]])
                kb_ev[buf] = None
                bank_sq = self.banks[0]
                self.mm_group(bank_sq, [(bank_sq.ap[:, 0:128], kbscr[buf],
                                         self.qT[h][:, c * TC:(c + 1) * TC])],
                              pre=[ev_kb, self.ev_q])
                kb_ev[buf] = bank_sq.w_ev
                ev_sq = self.evac(bank_sq, 'vector',
                                  lambda e, bank_sq=bank_sq: e.tensor_tensor(
                                      sqscr, bank_sq.ap[:, 0:128],
                                      self.v['triu01'], ALU.mult))
                ve.waits(getattr(self, 'ev_scaf2', []))
                ev_m2 = self.col_bcast_T(self.negA[:, ci:ci + 1], m2scr,
                                         pre=[self.evt("SE")])
                sc.wait(ev_m2)
                ev_d = sc.opb(lambda e, ci=ci: e.activation(
                    dscr, m2scr, AF.Exp, bias=self.gcolsA[:, ci:ci + 1]))
                ve.waits([ev_sq, ev_d])
                ev_sd = ve.opb(lambda e: e.tensor_tensor(sdscr, sqscr, dscr,
                                                         ALU.mult))
                ev_fb = self.col_bcast_T(self.facA[:, ci:ci + 1], fbscr)
                ev_qf = ve.opb(lambda e, h=h, c=c: e.tensor_tensor(
                    qfscr, self.qT[h][:, c * TC:(c + 1) * TC],
                    fbscr, ALU.mult))
                ve.opb(lambda e, h=h: e.tensor_copy(
                    cbf[:, 0:256], self.caug[:, h * DVW:(h + 1) * DVW]))
                ev_cbf = ve.opb(lambda e, h=h: e.tensor_copy(
                    cbf[:, 256:257], self.ncols[:, h:h + 1]))
                bank_h = self.banks[1]
                mms = [(bank_h.ap[:, 0:256], sdscr,
                        self.vsb[c][:, h * DVW:(h + 1) * DVW]),
                       (bank_h.ap[:, 256:257], sdscr, self.v['ones_b_col']),
                       (bank_h.ap[:, 0:256], qfscr, cbf[:, 0:256]),
                       (bank_h.ap[:, 256:257], qfscr, cbf[:, 256:257])]
                self.mm_group(bank_h, mms, pre=[ev_sd, ev_qf, ev_cbf])
                ve.wait(kw_ev[buf])
                ve.opb(lambda e, c=c, h=h, ci=ci, buf=buf: e.tensor_scalar(
                    kwscr[buf], self.ksb[c][:, h * DQW:(h + 1) * DQW],
                    self.wcolsA[:, ci:ci + 1], None, ALU.mult))
                bank_k = self.banks[2]
                self.mm_group(bank_k, [(bank_k.ap[:, 0:256], kwscr[buf],
                                        self.vsb[c][:, h * DVW:(h + 1) * DVW]),
                                       (bank_k.ap[:, 256:257], kwscr[buf],
                                        self.v['ones_b_col'])],
                              pre=[self.evt("VE")])
                kw_ev[buf] = bank_k.w_ev
                ve.opb(lambda e: e.tensor_copy(self.facL_cols[:, 0:1],
                                               fbscr[:, 127:128]))
                ve.wait(bank_h.w_ev)
                ve.opb(lambda e, bank_h=bank_h: e.tensor_copy(
                    self.qn_col, bank_h.ap[:, 256:257]))
                ve.opb(lambda e: e.scalar_tensor_tensor(
                    self.den_col, self.qn_col, -1.0, self.qn_col,
                    ALU.mult, ALU.max))
                ve.opb(lambda e, ci=ci: e.tensor_tensor(
                    self.den_col, self.den_col, self.emcolsA[:, ci:ci + 1],
                    ALU.max))
                ve.opb(lambda e: e.tensor_scalar_add(self.den_col,
                                                     self.den_col, EPS))
                ve.opb(lambda e: e.reciprocal(self.t3_col, self.den_col))
                ev_h = ve.opb(lambda e, bank_h=bank_h: e.tensor_scalar(
                    hsc, bank_h.ap[:, 0:256], self.t3_col, None, ALU.mult))
                bank_h.r_ev = ev_h
                ve.wait(bank_k.w_ev)
                ve.opb(lambda e, h=h, bank_k=bank_k: e.scalar_tensor_tensor(
                    self.caug[:, h * DVW:(h + 1) * DVW],
                    self.caug[:, h * DVW:(h + 1) * DVW],
                    self.facL_cols[:, 0:1], bank_k.ap[:, 0:256],
                    ALU.mult, ALU.add))
                ev_st = ve.opb(lambda e, h=h, bank_k=bank_k: e.scalar_tensor_tensor(
                    self.ncols[:, h:h + 1], self.ncols[:, h:h + 1],
                    self.facL_cols[:, 0:1], bank_k.ap[:, 256:257],
                    ALU.mult, ALU.add))
                bank_k.r_ev = ev_st
                ve.opb(lambda e: e.tensor_reduce(self.t_col, hsc, AX, ALU.add))
                ve.opb(lambda e: e.tensor_scalar_mul(self.mu_col, self.t_col,
                                                     1.0 / DVW))
                sc.wait(self.evt("VE"))
                ev_s2 = sc.opb(lambda e: e.activation(sqd, hsc, AF.Square,
                                                      accum_out=self.s2_col))
                ve.wait(ev_s2)
                ve.opb(lambda e: e.tensor_tensor(self.t2_col, self.mu_col,
                                                 self.mu_col, ALU.mult))
                ve.opb(lambda e: e.scalar_tensor_tensor(
                    self.t_col, self.s2_col, 1.0 / DVW, self.t2_col,
                    ALU.mult, ALU.subtract))
                ev_var = ve.opb(lambda e: e.tensor_scalar_max(
                    self.t_col, self.t_col, 0.0))
                sc.wait(ev_var)
                ev_sd2 = sc.opb(lambda e: e.activation(
                    self.t2_col, self.t_col, AF.Sqrt,
                    bias=self.v['eps_col']))
                ve.wait(ev_sd2)
                ve.opb(lambda e: e.reciprocal(self.rstd_col, self.t2_col))
                ve.opb(lambda e: e.tensor_scalar(
                    t256, hsc, self.mu_col, self.rstd_col,
                    ALU.subtract, ALU.mult))
                ve.op(lambda e, c=c, h=h: e.tensor_tensor(
                    self.ogg[c][:, h * DVW:(h + 1) * DVW], t256,
                    self.ogg[c][:, h * DVW:(h + 1) * DVW], ALU.mult))
        self.ev_scan2 = self.evt("VE")
        self.dbg('qT3post', self.qT[3], [self.ev_scan2])
        self.dbg('facApost', self.facA, [self.ev_scan2])
        self.dbg('negApost', self.negA, [self.ev_scan2])
        self.dbg('emApost', self.emcolsA, [self.ev_scan2])
        self.dbg('gApost', self.gcolsA, [self.ev_scan2])
        self.dbg('wApost', self.wcolsA, [self.ev_scan2])
        self.dbg('gout0', self.ogg[0], [self.ev_scan2])
        self.dbg('gout7', self.ogg[7], [self.ev_scan2])
        self.dbg('caug2', self.caug, [self.ev_scan2])

    def ph_wout(self):
        dr = self.dr
        ve, te, sc, gp, sy = (self.E[k] for k in
                              ('vector', 'tensor', 'scalar', 'gpsimd', 'sync'))
        ve.waits([self.evt("TE"), self.evt("SE"), self.evt("GL"),
                  self.evt("SL")])
        sy.waits([self.evt("VE"), self.evt("SE"), self.evt("TE")])
        xt = dr['x'].ap().rearrange("(n p) d -> n p d", p=128)
        ev_xs = []
        for ts in range(NCH):
            ev_xs.append(sy.op(lambda e, ts=ts: e.dma_start(
                out=self.x1[ts], in_=xt[ts]), inc=16))
        goT = self.tiles('B', BF16, KT, T)
        for c in range(NCH):
            for k in range(KT):
                self.tr_one(self.ogg[c][:, k * 128:(k + 1) * 128],
                            goT[k][:, c * 128:(c + 1) * 128],
                            pre=[self.ev_scan2], dst_pre=[self.ev_scan2])
        ev_goT = self.evt("VE")
        wb = self.tiles('WBUF', BF16, KT, 256, byteoff=0)
        ev_free = None
        for si in range(8):
            o0 = si * 256
            gp.wait(ev_free)
            gp.waits([self.evt("VE"), self.evt("SE")])
            src = dr['WoutT'].ap()[:, o0:o0 + 256].rearrange("(n p) o -> p n o",
                                                             p=128)
            d0 = wb[0]
            dst3 = rep(d0, [d0.ap[0], [256, KT], [1, 256]])
            ev_w = gp.op(lambda e, d=dst3, s=src: e.dma_start(out=d, in_=s),
                         inc=16)
            for ts in range(NCH):
                bank = self.banks[ts % 3]
                mms = [(bank.ap[:, 0:256],
                        goT[k][:, ts * 128:(ts + 1) * 128], wb[k])
                       for k in range(KT)]
                self.mm_group(bank, mms, pre=[ev_w, ev_goT])
                self.evac(bank, 'vector',
                          lambda e, ts=ts, o0=o0, bank=bank: e.tensor_tensor(
                              self.x1[ts][:, o0:o0 + 256],
                              self.x1[ts][:, o0:o0 + 256],
                              bank.ap[:, 0:256], ALU.add),
                          pre=[ev_xs[ts]])
            ev_free = self.evt("TE")
        self.ev_wout = self.evt("VE")
        self.dbg('x1a0', self.x1[0], [self.ev_wout])

    def row_sum(self, tiles_list, kn, dest_row, accumulate, pre=()):
        te = self.E['tensor']
        bank = self.banks[2]
        for half in range(2):
            te.wait(bank.r_ev)
            if half == 0:
                te.waits(pre)
            for kk in range(kn):
                osl = bank.ap[0:1, 0:512]
                rh = tiles_list[kk][:, half * 512:(half + 1) * 512]
                if kk == kn - 1:
                    bank.w_ev = te.op(lambda e, osl=osl, rh=rh, s=(kn == 1):
                                      e.matmul(osl, self.v['ones_b_col'], rh,
                                               start=s, stop=True))
                else:
                    te.raw(lambda e, osl=osl, rh=rh, s=(kk == 0):
                           e.matmul(osl, self.v['ones_b_col'], rh,
                                    start=s, stop=False))
            dsl_ = dest_row[0:1, half * 512:(half + 1) * 512]
            if accumulate:
                self.evac(bank, 'vector',
                          lambda e, dsl_=dsl_, bank=bank: e.tensor_add(
                              dsl_, dsl_, bank.ap[0:1, 0:512]))
            else:
                self.evac(bank, 'vector',
                          lambda e, dsl_=dsl_, bank=bank: e.tensor_copy(
                              dsl_, bank.ap[0:1, 0:512]))

    def ph_ffn0(self):
        dr = self.dr
        ve, te, sc, gp, sy = (self.E[k] for k in
                              ('vector', 'tensor', 'scalar', 'gpsimd', 'sync'))
        GRP = 16
        sgscr = self.view('SCR', BF16, [128, 512], byteoff=0)
        u0scr = self.view('SCR', F32, [128, 512], byteoff=1024)
        m128 = self.view('SCR', BF16, [1, 128], byteoff=3072)
        shcol = self.view('SCR', F32, [128, 1], byteoff=3328)
        sumx = self.view('SCR', F32, [1, 1024], byteoff=4096)
        sxb = self.view('SMALL', BF16, [1, 1024], byteoff=0)
        sumh = self.view('WBUF', F32, [1, 1024], byteoff=0)
        dsr = self.view('WBUF', BF16, [1, 2048], byteoff=4096)
        dsb = self.view('WBUF', BF16, [128, 2048], byteoff=8192)
        hffT = self.tiles('C', BF16, GRP, T)
        wgbG = [self.tiles('B', BF16, KT, 128, byteoff=b * 4096)
                for b in range(2)]
        wgbU = [self.tiles('B', BF16, KT, 128, byteoff=8192 + b * 4096)
                for b in range(2)]
        qdblk = self.tiles('B', BF16, GRP, 512, byteoff=16384)
        self.row_sum(self.xnT, KT, sumx, accumulate=False)
        gp.waits([self.evt("VE"), self.evt("TE")])
        gp.op(lambda e: e.memset(m128, -128.0), sem="GP")
        ve.wait(self.evt("GP"))
        ve.opb(lambda e: e.tensor_copy(sxb, sumx))
        ev_dsr = gp.op(lambda e: e.dma_start(out=dsr, in_=dr['dscr'].ap()),
                       inc=16)
        for j in range(4):
            bank = self.banks[j % 2]
            self.mm_group(bank, [(bank.ap[:, 0:512],
                                  self.v['ones_b_row'][0:1, 0:128],
                                  dsr[0:1, j * 512:(j + 1) * 512])],
                          pre=[ev_dsr])
            self.evac(bank, 'vector',
                      lambda e, j=j, bank=bank: e.tensor_copy(
                          dsb[:, j * 512:(j + 1) * 512], bank.ap[:, 0:512]))
        self.dbg('sumx', sumx, [self.evt("VE")])
        self.dbg('dsb', dsb, [self.evt("VE")])
        ev_wg_free = [None, None]
        for g in range(4):
            k0 = g * GRP
            for jj in range(GRP):
                isl = k0 + jj
                b = isl % 2
                gp.wait(ev_wg_free[b])
                gp.wait(self.evt("VE"))
                srcg = dr['QgT'].ap()[:, isl * 128:(isl + 1) * 128].rearrange(
                    "(n p) o -> p n o", p=128)
                srcu = dr['QuT'].ap()[:, isl * 128:(isl + 1) * 128].rearrange(
                    "(n p) o -> p n o", p=128)
                dg, du = wgbG[b][0], wgbU[b][0]
                ev_wgl = gp.op(lambda e, d=rep(dg, [dg.ap[0], [128, KT],
                                                    [1, 128]]), s=srcg:
                               e.dma_start(out=d, in_=s), inc=16)
                ev_wul = gp.op(lambda e, d=rep(du, [du.ap[0], [128, KT],
                                                    [1, 128]]), s=srcu:
                               e.dma_start(out=d, in_=s), inc=16)
                for th in range(2):
                    bg, bu = self.banks[0], self.banks[1]
                    mmsg = [(bg.ap[:, 0:512], wgbG[b][k],
                             self.xnT[k][:, th * 512:(th + 1) * 512])
                            for k in range(KT)]
                    mmsg.append((bg.ap[:, 0:512], m128,
                                 sxb[0:1, th * 512:(th + 1) * 512]))
                    self.mm_group(bg, mmsg, pre=[ev_wgl, self.evt("GP")])
                    mmsu = [(bu.ap[:, 0:512], wgbU[b][k],
                             self.xnT[k][:, th * 512:(th + 1) * 512])
                            for k in range(KT)]
                    mmsu.append((bu.ap[:, 0:512], m128,
                                 sxb[0:1, th * 512:(th + 1) * 512]))
                    self.mm_group(bu, mmsu, pre=[ev_wul])
                    ev_sg = self.evac(bg, 'scalar',
                                      lambda e, isl=isl, bg=bg:
                                      e.activation(
                                          sgscr, bg.ap[:, 0:512], AF.Silu,
                                          scale=self.v['gsc'][:, isl:isl + 1]),
                                      pre=[self.evt("VE")])
                    self.evac(bu, 'vector',
                              lambda e, isl=isl, bu=bu: e.tensor_scalar(
                                  u0scr, bu.ap[:, 0:512],
                                  self.v['usc'][:, isl:isl + 1],
                                  None, ALU.mult))
                    ve.wait(ev_sg)
                    ve.op(lambda e, jj=jj, th=th: e.tensor_tensor(
                        hffT[jj][:, th * 512:(th + 1) * 512],
                        sgscr, u0scr, ALU.mult))
                ev_wg_free[b] = self.evt("TE")
            ev_hff = self.evt("VE")
            if g == 0:
                self.dbg('hff0', hffT[0], [ev_hff])
                self.dbg('hff15', hffT[15], [ev_hff])
            self.row_sum(hffT, GRP, sumh, accumulate=(g > 0), pre=[ev_hff])
            for dsl in range(4):
                gp.waits([self.evt("TE"), self.evt("VE")])
                src = dr['QdT'].ap()[k0 * 128:(k0 + GRP) * 128,
                                     dsl * 512:(dsl + 1) * 512].rearrange(
                    "(n p) o -> p n o", p=128)
                d0 = qdblk[0]
                ev_qd = gp.op(lambda e, d=rep(d0, [d0.ap[0], [512, GRP],
                                                   [1, 512]]), s=src:
                              e.dma_start(out=d, in_=s), inc=16)
                for ts in range(NCH):
                    bank = self.banks[ts % 2]
                    mms = [(bank.ap[:, 0:512],
                            hffT[kk][:, ts * 128:(ts + 1) * 128], qdblk[kk])
                           for kk in range(GRP)]
                    self.mm_group(bank, mms, pre=[ev_qd, ev_hff])
                    self.evac(bank, 'vector',
                              lambda e, dsl=dsl, bank=bank:
                              e.tensor_tensor(
                                  u0scr, bank.ap[:, 0:512],
                                  dsb[:, dsl * 512:(dsl + 1) * 512], ALU.mult))
                    ve.op(lambda e, ts=ts, dsl=dsl: e.tensor_tensor(
                        self.x1[ts][:, dsl * 512:(dsl + 1) * 512],
                        self.x1[ts][:, dsl * 512:(dsl + 1) * 512],
                        u0scr, ALU.add))
        self.dbg('sumh', sumh, [self.evt("VE")])
        for ts in range(NCH):
            te.wait(self.pssb.r_ev)
            te.wait(self.evt("VE"))
            self.pssb.w_ev = te.op(lambda e, ts=ts: e.matmul(
                self.pssb.ap[:, 300:301],
                sumh[0:1, ts * 128:(ts + 1) * 128],
                self.v['ones_f_row'][0:1, 0:1], start=True, stop=True))
            ev_sh = self.evac(self.pssb, 'vector',
                              lambda e: e.tensor_scalar_mul(
                                  shcol, self.pssb.ap[:, 300:301], -128.0))
            ve.wait(ev_sh)
            for dsl in range(4):
                ve.op(lambda e, ts=ts, dsl=dsl: e.scalar_tensor_tensor(
                    self.x1[ts][:, dsl * 512:(dsl + 1) * 512],
                    dsb[:, dsl * 512:(dsl + 1) * 512], shcol,
                    self.x1[ts][:, dsl * 512:(dsl + 1) * 512],
                    ALU.mult, ALU.add))
        self.ev_ffn0 = self.evt("VE")
        self.dbg('x1f0', self.x1[0], [self.ev_ffn0])
        self.dbg('x1f7', self.x1[7], [self.ev_ffn0])

    def ph_l1pre(self):
        dr = self.dr
        ve, te, sc, gp, sy = (self.E[k] for k in
                              ('vector', 'tensor', 'scalar', 'gpsimd', 'sync'))
        ve.waits([self.evt("TE"), self.evt("SE"), self.evt("GL"),
                  self.evt("SL")])
        sc.waits([self.evt("VE"), self.evt("TE")])
        sy.waits([self.evt("VE"), self.evt("SE"), self.evt("TE")])
        xn1f = self.view('C', F32, [128, 2048], byteoff=0)
        xnb = self.view('C', BF16, [128, 2048], byteoff=8192)
        hhb = self.view('C', BF16, [128, 2048], byteoff=12288)
        sqd = self.view('C', F32, [128, 1024], byteoff=16384)
        self.hT = self.tiles('B', BF16, KT, T)
        self.ev_x1out = []
        for ts in range(NCH):
            x1t = self.x1[ts]
            ve.opb(lambda e, x1t=x1t: e.tensor_reduce(self.t_col, x1t, AX,
                                                      ALU.add))
            ve.opb(lambda e: e.tensor_scalar_mul(self.mu_col, self.t_col,
                                                 1.0 / D))
            sc.wait(self.evt("VE"))
            sc.raw(lambda e, x1t=x1t: e.activation(sqd, x1t[:, 0:1024],
                                                   AF.Square,
                                                   accum_out=self.s2_col))
            ev_s2 = sc.op(lambda e, x1t=x1t: e.activation(
                sqd, x1t[:, 1024:2048], AF.Square, accum_out=self.t2_col))
            ve.wait(ev_s2)
            ve.opb(lambda e: e.tensor_add(self.s2_col, self.s2_col,
                                          self.t2_col))
            ve.opb(lambda e: e.tensor_tensor(self.t2_col, self.mu_col,
                                             self.mu_col, ALU.mult))
            ev_var = ve.opb(lambda e: e.scalar_tensor_tensor(
                self.t_col, self.s2_col, 1.0 / D, self.t2_col,
                ALU.mult, ALU.subtract))
            sc.wait(ev_var)
            ev_sd = sc.op(lambda e: e.activation(self.t2_col, self.t_col,
                                                 AF.Sqrt,
                                                 bias=self.v['lneps_col']))
            ve.wait(ev_sd)
            ve.opb(lambda e: e.reciprocal(self.rstd_col, self.t2_col))
            ve.wait(self.evt("TE"))
            ev_xn = ve.op(lambda e, x1t=x1t: e.tensor_scalar(
                xn1f, x1t, self.mu_col, self.rstd_col,
                ALU.subtract, ALU.mult))
            ev_xnb = ve.op(lambda e: e.tensor_copy(xnb, xn1f))
            sy.wait(ev_xnb)
            sy.op(lambda e, ts=ts: e.dma_start(
                out=self.xn1d.ap()[ts * 128:(ts + 1) * 128, :], in_=xnb),
                inc=16)
            self.ev_x1out.append(sy.op(lambda e, ts=ts, x1t=x1t: e.dma_start(
                out=self.out.ap()[ts * 128:(ts + 1) * 128, :], in_=x1t),
                inc=16))
            sc.wait(ev_xn)
            sc.raw(lambda e: e.activation(sqd, xn1f[:, 0:1024], AF.Square,
                                          accum_out=self.s2_col))
            ev_q2 = sc.op(lambda e: e.activation(sqd, xn1f[:, 1024:2048],
                                                 AF.Square,
                                                 accum_out=self.t2_col))
            ve.wait(ev_q2)
            ve.opb(lambda e: e.tensor_add(self.s2_col, self.s2_col,
                                          self.t2_col))
            ve.opb(lambda e: e.tensor_scalar_mul(self.s2_col, self.s2_col,
                                                 1.0 / D))
            sc.wait(self.evt("VE"))
            ev_r = sc.op(lambda e: e.activation(self.t2_col, self.s2_col,
                                                AF.Sqrt,
                                                bias=self.v['eps_col']))
            ve.wait(ev_r)
            ve.opb(lambda e: e.reciprocal(self.rstd_col, self.t2_col))
            ev_hh = ve.op(lambda e: e.tensor_scalar(
                hhb, xn1f, self.rstd_col, None, ALU.mult))
            for k in range(KT):
                self.tr_one(xnb[:, k * 128:(k + 1) * 128],
                            self.xnT[k][:, ts * 128:(ts + 1) * 128],
                            pre=[ev_xnb])
                self.tr_one(hhb[:, k * 128:(k + 1) * 128],
                            self.hT[k][:, ts * 128:(ts + 1) * 128],
                            pre=[ev_hh],
                            scale=self.v['pnwT'][:, k:k + 1])
        self.ev_l1pre = self.evt("VE")
        self.dbg('xn1T0', self.xnT[0], [self.ev_l1pre])
        self.dbg('hT0', self.hT[0], [self.ev_l1pre])
        self.dbg('hT15', self.hT[15], [self.ev_l1pre])

    def ph_w2prep(self):
        dr = self.dr
        ve, te, sc, gp, sy = (self.E[k] for k in
                              ('vector', 'tensor', 'scalar', 'gpsimd', 'sync'))
        wprep = [self.view('WBUF', BF16, [128, 2048], byteoff=b * 4096)
                 for b in range(2)]
        w2s = self.view('WBUF', BF16, [128, 2048], byteoff=8192)
        gp.waits([self.evt("TE"), self.evt("VE"), self.evt("SE")])
        sy.wait(self.evt("VE"))
        ev_free = [None, None]
        ev_st = None
        SC = 2.0 ** -6.5
        for k in range(64):
            b = k % 2
            gp.wait(ev_free[b])
            src = dr['W2Tp'].ap()[k * 128:(k + 1) * 128, :]
            ev_w = gp.op(lambda e, b=b, src=src: e.dma_start(out=wprep[b],
                                                             in_=src), inc=16)
            for n in range(4):
                bank = self.banks[n % 2]
                self.mm_group(bank, [(bank.ap[:, 0:512], self.v['I2H64'],
                                      wprep[b][:, n * 512:(n + 1) * 512])],
                              pre=[ev_w, self.ev_consts[0], ev_st])
                self.evac(bank, 'vector',
                          lambda e, n=n, bank=bank: e.tensor_scalar_mul(
                              w2s[:, n * 512:(n + 1) * 512],
                              bank.ap[:, 0:512], SC))
            ev_free[b] = self.evt("TE")
            ev_v = self.evt("VE")
            sy.wait(ev_v)
            ev_st = sy.op(lambda e, k=k: e.dma_start(
                out=self.w2b.ap()[k * 128:(k + 1) * 128, :], in_=w2s), inc=16)
        self.ev_w2prep = ev_st

    def ph_l1attn(self):
        dr = self.dr
        ve, te, sc, gp, sy = (self.E[k] for k in
                              ('vector', 'tensor', 'scalar', 'gpsimd', 'sync'))
        A0 = self.view('C', F32, [128, 1024], byteoff=0)
        A1 = self.view('C', F32, [128, 1024], byteoff=4096)
        A2 = self.view('C', F32, [128, 1024], byteoff=8192)
        A3 = self.view('C', F32, [128, 1024], byteoff=12288)
        self.REC = self.tiles('REC', BF16, KT, T)
        self.CDG = self.tiles('CDG', BF16, KT, T)
        wab = self.tiles('WBUF', BF16, KT, 128, byteoff=0)
        wib = self.tiles('WBUF', BF16, KT, 128, byteoff=4096)
        wvb = self.tiles('WBUF', BF16, KT, 128, byteoff=8192)
        sv = lambda off, w=16: self.view('SCR', F32, [128, w], byteoff=off)
        s_inc, sB, p_inc, pE, z16 = sv(0), sv(64), sv(128), sv(192), sv(256)
        S, T1, T2 = sv(320), sv(384), sv(448)
        ccin = sv(512, 32)
        ccout = sv(1024, 128)
        w64 = lambda ap: ap.rearrange("p (a b) -> p a b", b=64)
        sBb = dataclasses.replace(sB, ap=[sB.ap[0], [1, 16], [0, 64]])
        pEb = dataclasses.replace(pE, ap=[pE.ap[0], [1, 16], [0, 64]])
        ve.waits([self.evt("TE"), self.evt("SE"), self.evt("SL"),
                  self.evt("GL")])
        sy.waits([self.evt("VE"), self.evt("SE"), self.evt("TE")])
        gp.waits([self.evt("VE"), self.evt("SE")])
        ev_mask = sy.op(lambda e: e.dma_start(out=A3,
                                              in_=dr['mask0_64'].ap()), inc=16)
        ve.opb(lambda e: e.memset(z16, 0.0))
        stg = self.cfg.get('l1a_stage', 99)
        nk16 = self.cfg.get('l1a_nk', KT)
        evs_w = [None, None, None]
        for k16 in range(nk16):
            gp.wait(self.evt("TE"))
            for wi_, (wname, bufs) in enumerate((('wa', wab), ('wi', wib),
                                                 ('wv', wvb))):
                src = dr[wname].ap()[:, k16 * 128:(k16 + 1) * 128].rearrange(
                    "(n p) o -> p n o", p=128)
                d0 = bufs[0]
                dst3 = rep(d0, [d0.ap[0], [128, KT], [1, 128]])
                evs_w[wi_] = gp.op(lambda e, d=dst3, s=src:
                                   e.dma_start(out=d, in_=s), inc=16)
            jobs = [(wab, A0, 'a', evs_w[0]), (wib, A1, 'i', evs_w[1]),
                    (wvb, A2, 'v', evs_w[2])]
            bi = 0
            for bufs, dstf, kind, evw in jobs:
                for h in range(2):
                    bank = self.banks[bi % 3]
                    bi += 1
                    mms = [(bank.ap[:, 0:512], bufs[k],
                            self.hT[k][:, h * 512:(h + 1) * 512])
                           for k in range(KT)]
                    self.mm_group(bank, mms, pre=[evw, self.ev_l1pre])
                    if kind == 'a':
                        self.evac(bank, 'scalar',
                                  lambda e, h=h, bank=bank, k16=k16:
                                  e.activation(
                                      A0[:, h * 512:(h + 1) * 512],
                                      bank.ap[:, 0:512], AF.Sigmoid,
                                      bias=self.v['dbias'][:, k16:k16 + 1]),
                                  pre=[self.evt("VE")])
                    elif kind == 'i':
                        self.evac(bank, 'scalar',
                                  lambda e, h=h, bank=bank: e.activation(
                                      A1[:, h * 512:(h + 1) * 512],
                                      bank.ap[:, 0:512], AF.Sigmoid,
                                      bias=self.v['zeros_col']),
                                  pre=[self.evt("VE")])
                    else:
                        self.evac(bank, 'vector',
                                  lambda e, h=h, bank=bank: e.tensor_copy(
                                      A2[:, h * 512:(h + 1) * 512],
                                      bank.ap[:, 0:512]),
                                  pre=[self.evt("SE")])
            if stg < 2:
                continue
            ve.wait(self.evt("SE"))
            ve.opb(lambda e: e.tensor_tensor(A1, A1, A2, ALU.mult))
            ve.opb(lambda e: e.tensor_tensor(A2, A0, A0, ALU.mult))
            ve.opb(lambda e: e.tensor_scalar(A2, A2, -1.0, 1.0,
                                             ALU.mult, ALU.add))
            ve.opb(lambda e: e.tensor_scalar_max(A2, A2, 1e-8))
            sc.wait(self.evt("VE"))
            ev_sq = sc.op(lambda e: e.activation(A2, A2, AF.Sqrt,
                                                 bias=self.v['zeros_col']))
            ve.wait(ev_sq)
            ve.opb(lambda e: e.tensor_tensor(A1, A1, A2, ALU.mult))
            ve.opb(lambda e: e.tensor_scalar_max(A2, A0, 1e-10))
            sc.wait(self.evt("VE"))
            ev_ln = sc.op(lambda e: e.activation(A2, A2, AF.Ln,
                                                 bias=self.v['zeros_col']))
            ve.wait(ev_ln)
            ve.wait(ev_mask)
            ve.opb(lambda e: e.tensor_tensor_scan(A0, A3, A2, 0.0,
                                                  ALU.mult, ALU.add))
            sc.wait(self.evt("VE"))
            ev_cd = sc.op(lambda e: e.activation(A2, A0, AF.Exp,
                                                 bias=self.v['zeros_col']))
            ve.wait(ev_cd)
            ve.opb(lambda e: e.tensor_scalar_max(A0, A2, 1e-10))
            ve.opb(lambda e: e.reciprocal(A0, A0))
            ve.opb(lambda e: e.tensor_tensor(A0, A1, A0, ALU.mult))
            ve.opb(lambda e: e.tensor_tensor_scan(A1, A3, A0, 0.0,
                                                  ALU.mult, ALU.add))
            ve.opb(lambda e: e.tensor_tensor(A1, A2, A1, ALU.mult))
            if stg < 3:
                continue
            ve.opb(lambda e: e.tensor_tensor_scan(
                s_inc, A2[:, 63::64], A1[:, 63::64], 0.0,
                ALU.mult, ALU.add))
            ve.opb(lambda e: e.memset(sB[:, 0:1], 0.0))
            ve.opb(lambda e: e.tensor_copy(sB[:, 1:16], s_inc[:, 0:15]))
            ve.opb(lambda e: e.tensor_tensor_scan(
                p_inc, A2[:, 63::64], z16, 1.0, ALU.mult, ALU.add))
            ve.opb(lambda e: e.memset(pE[:, 0:1], 1.0))
            ve.opb(lambda e: e.tensor_copy(pE[:, 1:16], p_inc[:, 0:15]))
            if stg < 4:
                continue
            sub = self.cfg.get('l1a_sub', 4)
            ve.opb(lambda e, sBb=sBb: e.tensor_tensor(w64(A0), w64(A2), sBb,
                                                      ALU.mult))
            if sub >= 2:
                ve.op(lambda e, k16=k16: e.tensor_tensor(self.REC[k16], A1, A0,
                                                         ALU.add))
            if sub >= 3:
                ve.op(lambda e, k16=k16, pEb=pEb: e.tensor_tensor(
                    w64(self.CDG[k16]), w64(A2), pEb, ALU.mult))
            if sub >= 4:
                ve.opb(lambda e, k16=k16: e.tensor_copy(
                    ccin[:, k16:k16 + 1], self.REC[k16][:, 1023:1024]))
                ve.opb(lambda e, k16=k16: e.tensor_copy(
                    ccin[:, 16 + k16:17 + k16], self.CDG[k16][:, 1023:1024]))
        if stg < 5:
            self.ev_l1scan = self.evt("VE")
            ev = [self.ev_l1scan]
            self.dbg('la0', A0, ev); self.dbg('la1', A1, ev)
            self.dbg('la2', A2, ev); self.dbg('lsinc', s_inc, ev)
            self.dbg('lsb', sB, ev); self.dbg('lpinc', p_inc, ev)
            self.dbg('lpe', pE, ev)
            self.dbg('rec0', self.REC[0], ev)
            self.dbg('cdg0', self.CDG[0], ev)
            self.dbg('rec3', self.REC[3], ev)
            self.dbg('cdg3', self.CDG[3], ev)
            self.dbg('rec4', self.REC[4], ev)
            self.dbg('cdg4', self.CDG[4], ev)
            return
        sy.wait(self.evt("VE"))
        ev_ccs = sy.op(lambda e: e.dma_start(out=self.cc1_in.ap(), in_=ccin),
                       inc=16)
        gp.wait(ev_ccs)
        gp.wait(getattr(self, 'ev_w2prep', None))
        self.ev_cc1 = gp.op(lambda e: e.collective_compute(
            "AllGather", ALU.bypass,
            replica_groups=[[0, 1, 2, 3], [4, 5, 6, 7]],
            ins=[self.cc1_in.ap()], outs=[self.cc1_out.ap()]),
            inc=1, sem="CC")
        sy.wait(self.ev_cc1)
        ev_cco = sy.op(lambda e: e.dma_start(
            out=ccout.rearrange("p (r c) -> p r c", c=32),
            in_=self.cc1_out.ap().rearrange("(r p) c -> p r c", p=128)),
            inc=16)
        ve.wait(ev_cco)
        ve.opb(lambda e: e.memset(S, 0.0))
        for k in range(3):
            lastk = ccout[:, k * 32:k * 32 + 16]
            cdgk = ccout[:, k * 32 + 16:k * 32 + 32]
            ve.opb(lambda e, cdgk=cdgk: e.tensor_tensor(T1, cdgk, S,
                                                        ALU.mult))
            ve.opb(lambda e, lastk=lastk: e.tensor_add(T1, T1, lastk))
            ve.opb(lambda e: e.tensor_sub(T2, T1, S))
            ve.opb(lambda e, k=k: e.scalar_tensor_tensor(
                S, T2, self.v['fsel128'][:, k:k + 1], S, ALU.mult, ALU.add))
        for k16 in range(KT):
            ve.op(lambda e, k16=k16: e.scalar_tensor_tensor(
                self.REC[k16], self.CDG[k16], S[:, k16:k16 + 1],
                self.REC[k16], ALU.mult, ALU.add))
        self.ev_l1scan = self.evt("VE")
        self.dbg('rec0', self.REC[0], [self.ev_l1scan])
        self.dbg('rec15', self.REC[15], [self.ev_l1scan])
        self.dbg('cdg0', self.CDG[0], [self.ev_l1scan])

    def ph_l1post(self):
        dr = self.dr
        ve, te, sc, gp, sy = (self.E[k] for k in
                              ('vector', 'tensor', 'scalar', 'gpsimd', 'sync'))
        opwb = self.tiles('WBUF', BF16, KT, 256, byteoff=0)
        xnr = [self.view('WBUF', BF16, [128, 256], byteoff=8192 + b * 512)
               for b in range(2)]
        acc = self.tiles('B', BF16, NCH, D)
        ev_free = None
        for si in range(8):
            o0 = si * 256
            gp.wait(ev_free)
            gp.waits([self.evt("CC"), self.evt("TE"), self.evt("VE")])
            src = dr['opw'].ap()[:, o0:o0 + 256].rearrange("(n p) o -> p n o",
                                                           p=128)
            d0 = opwb[0]
            dst3 = rep(d0, [d0.ap[0], [256, KT], [1, 256]])
            ev_w = gp.op(lambda e, d=dst3, s=src: e.dma_start(out=d, in_=s),
                         inc=16)
            for ts in range(NCH):
                bank = self.banks[(si * NCH + ts) % 3]
                mms = [(bank.ap[:, 0:256],
                        self.REC[k][:, ts * 128:(ts + 1) * 128], opwb[k])
                       for k in range(KT)]
                self.mm_group(bank, mms, pre=[ev_w, self.ev_l1scan])
                b = ts % 2
                sy.wait(self.evt("VE"))
                ev_x = sy.op(lambda e, ts=ts, o0=o0, b=b: e.dma_start(
                    out=xnr[b],
                    in_=self.xn1d.ap()[ts * 128:(ts + 1) * 128, o0:o0 + 256]),
                    inc=16)
                self.evac(bank, 'vector',
                          lambda e, ts=ts, o0=o0, b=b, bank=bank:
                          e.tensor_tensor(acc[ts][:, o0:o0 + 256], xnr[b],
                                          bank.ap[:, 0:256], ALU.add),
                          pre=[ev_x])
            ev_free = self.evt("TE")
        self.dbg('acc0', acc[0], [self.evt("VE")])
        sy.waits([self.evt("TE"), self.evt("VE")])
        for ts in range(NCH):
            ev_r = sy.op(lambda e, ts=ts: e.dma_start(
                out=self.x1[ts],
                in_=self.out.ap()[ts * 128:(ts + 1) * 128, :]), inc=16)
            ve.wait(ev_r)
            ve.op(lambda e, ts=ts: e.tensor_tensor(
                self.x1[ts], self.x1[ts], acc[ts], ALU.add))
        dump = self.view('WBUF', F32, [128, 1024], byteoff=0)
        h2b = self.view('WBUF', BF16, [128, 2048], byteoff=4096)
        h2T = self.tiles('C', BF16, KT, T)
        for ts in range(NCH):
            sc.waits([self.evt("VE"), self.evt("TE")])
            sc.raw(lambda e, ts=ts: e.activation(dump, acc[ts][:, 0:1024],
                                                 AF.Square,
                                                 accum_out=self.s2_col))
            ev_q2 = sc.op(lambda e, ts=ts: e.activation(
                dump, acc[ts][:, 1024:2048], AF.Square,
                accum_out=self.t2_col))
            ve.wait(ev_q2)
            ve.opb(lambda e: e.tensor_add(self.s2_col, self.s2_col,
                                          self.t2_col))
            ve.opb(lambda e: e.tensor_scalar_mul(self.s2_col, self.s2_col,
                                                 1.0 / D))
            sc.wait(self.evt("VE"))
            ev_r = sc.op(lambda e: e.activation(self.t2_col, self.s2_col,
                                                AF.Sqrt,
                                                bias=self.v['eps_col']))
            ve.wait(ev_r)
            ve.opb(lambda e: e.reciprocal(self.rstd_col, self.t2_col))
            ve.wait(self.evt("TE"))
            ev_h2 = ve.op(lambda e, ts=ts: e.tensor_scalar(
                h2b, acc[ts], self.rstd_col, None, ALU.mult))
            for k in range(KT):
                self.tr_one(h2b[:, k * 128:(k + 1) * 128],
                            h2T[k][:, ts * 128:(ts + 1) * 128],
                            pre=[ev_h2],
                            scale=self.v['fnwT'][:, k:k + 1])
        ev_h2T = self.evt("VE")
        self.dbg('h2T0', h2T[0], [ev_h2T])
        wgub = self.tiles('WBUF', BF16, KT, 256, byteoff=0)
        hmid = self.tiles('B', BF16, 8, T)
        wdnb = self.tiles('B', BF16, 8, 512, byteoff=16384)
        sgp = self.view('SCR', BF16, [128, 512], byteoff=2048)
        for g in range(8):
            for jj in range(8):
                isl = g * 8 + jj
                gp.waits([self.evt("TE"), self.evt("SE")])
                src = dr['wguI'].ap()[:, isl * 256:(isl + 1) * 256].rearrange(
                    "(n p) o -> p n o", p=128)
                d0 = wgub[0]
                dst3 = rep(d0, [d0.ap[0], [256, KT], [1, 256]])
                ev_w = gp.op(lambda e, d=dst3, s=src: e.dma_start(out=d,
                                                                  in_=s),
                             inc=16)
                for th in range(2):
                    bg, bu = self.banks[0], self.banks[1]
                    mmsg = [(bg.ap[:, 0:512], wgub[k][:, 0:128],
                             h2T[k][:, th * 512:(th + 1) * 512])
                            for k in range(KT)]
                    self.mm_group(bg, mmsg, pre=[ev_w, ev_h2T])
                    mmsu = [(bu.ap[:, 0:512], wgub[k][:, 128:256],
                             h2T[k][:, th * 512:(th + 1) * 512])
                            for k in range(KT)]
                    self.mm_group(bu, mmsu)
                    ev_sg = self.evac(bg, 'scalar',
                                      lambda e, bg=bg: e.activation(
                                          sgp, bg.ap[:, 0:512], AF.Silu,
                                          bias=self.v['zeros_col']),
                                      pre=[self.evt("VE")])
                    ve.wait(ev_sg)
                    self.evac(bu, 'vector',
                              lambda e, jj=jj, th=th, bu=bu: e.tensor_tensor(
                                  hmid[jj][:, th * 512:(th + 1) * 512],
                                  sgp, bu.ap[:, 0:512], ALU.mult))
            ev_hmid = self.evt("VE")
            for dsl in range(4):
                gp.waits([self.evt("TE"), self.evt("VE")])
                src = dr['wdn'].ap()[g * 1024:(g + 1) * 1024,
                                     dsl * 512:(dsl + 1) * 512].rearrange(
                    "(n p) o -> p n o", p=128)
                d0 = wdnb[0]
                dst3 = rep(d0, [d0.ap[0], [512, 8], [1, 512]])
                ev_qd = gp.op(lambda e, d=dst3, s=src: e.dma_start(out=d,
                                                                   in_=s),
                              inc=16)
                for ts in range(NCH):
                    bank = self.banks[2 if ts % 2 else 0]
                    mms = [(bank.ap[:, 0:512],
                            hmid[kk][:, ts * 128:(ts + 1) * 128], wdnb[kk])
                           for kk in range(8)]
                    self.mm_group(bank, mms, pre=[ev_qd, ev_hmid])
                    self.evac(bank, 'vector',
                              lambda e, ts=ts, dsl=dsl, bank=bank:
                              e.tensor_tensor(
                                  self.x1[ts][:, dsl * 512:(dsl + 1) * 512],
                                  self.x1[ts][:, dsl * 512:(dsl + 1) * 512],
                                  bank.ap[:, 0:512], ALU.add))
        self.ev_l1post = self.evt("VE")
        self.dbg('x1p0', self.x1[0], [self.ev_l1post])

    def ph_l1ffn(self):
        dr = self.dr
        ve, te, sc, gp, sy = (self.E[k] for k in
                              ('vector', 'tensor', 'scalar', 'gpsimd', 'sync'))
        w1b = self.tiles('WBUF', BF16, KT, 128, byteoff=0)
        w3b = self.tiles('WBUF', BF16, KT, 128, byteoff=4096)
        hidb = self.tiles('B', BF16, 8, T)
        w2bb = self.tiles('B', BF16, 8, 512, byteoff=16384)
        sgp = self.view('SCR', BF16, [128, 512], byteoff=2048)
        w2r = self.w2b.ap().rearrange("(p c) d -> c p d", c=64)
        for g in range(8):
            for jj in range(8):
                isl = g * 8 + jj
                gp.waits([self.evt("TE"), self.evt("SE")])
                s1 = dr['W1T'].ap()[:, isl * 128:(isl + 1) * 128].rearrange(
                    "(n p) o -> p n o", p=128)
                s3 = dr['W3T'].ap()[:, isl * 128:(isl + 1) * 128].rearrange(
                    "(n p) o -> p n o", p=128)
                d1, d3 = w1b[0], w3b[0]
                ev_w1 = gp.op(lambda e, d=rep(d1, [d1.ap[0], [128, KT],
                                                   [1, 128]]), s=s1:
                              e.dma_start(out=d, in_=s), inc=16)
                ev_w3 = gp.op(lambda e, d=rep(d3, [d3.ap[0], [128, KT],
                                                   [1, 128]]), s=s3:
                              e.dma_start(out=d, in_=s), inc=16)
                for th in range(2):
                    bg, bu = self.banks[0], self.banks[1]
                    mmsg = [(bg.ap[:, 0:512], w1b[k],
                             self.xnT[k][:, th * 512:(th + 1) * 512])
                            for k in range(KT)]
                    self.mm_group(bg, mmsg, pre=[ev_w1])
                    mmsu = [(bu.ap[:, 0:512], w3b[k],
                             self.xnT[k][:, th * 512:(th + 1) * 512])
                            for k in range(KT)]
                    self.mm_group(bu, mmsu, pre=[ev_w3])
                    ev_sg = self.evac(bg, 'scalar',
                                      lambda e, bg=bg: e.activation(
                                          sgp, bg.ap[:, 0:512], AF.Silu,
                                          bias=self.v['zeros_col']),
                                      pre=[self.evt("VE")])
                    ve.wait(ev_sg)
                    self.evac(bu, 'vector',
                              lambda e, jj=jj, th=th, bu=bu: e.tensor_tensor(
                                  hidb[jj][:, th * 512:(th + 1) * 512],
                                  sgp, bu.ap[:, 0:512], ALU.mult))
            for jj in range(8):
                for th in range(2):
                    bank = self.banks[2]
                    self.mm_group(bank, [(bank.ap[:, 0:512], self.v['H128'],
                                          hidb[jj][:, th * 512:(th + 1) * 512])],
                                  pre=[self.ev_consts[0], self.evt("VE")])
                    self.evac(bank, 'vector',
                              lambda e, jj=jj, th=th, bank=bank:
                              e.tensor_copy(
                                  hidb[jj][:, th * 512:(th + 1) * 512],
                                  bank.ap[:, 0:512]))
            ev_hid = self.evt("VE")
            for dsl in range(4):
                sy.waits([self.evt("TE"), self.evt("VE")])
                src = w2r[g * 8:(g + 1) * 8, :, dsl * 512:(dsl + 1) * 512]
                d0 = w2bb[0]
                dst3 = rep(d0, [d0.ap[0], [512, 8], [1, 512]])
                ev_w2 = sy.op(lambda e, d=dst3, s=src: e.dma_start(out=d,
                                                                   in_=s),
                              inc=16)
                for ts in range(NCH):
                    bank = self.banks[0 if ts % 2 else 1]
                    mms = [(bank.ap[:, 0:512],
                            hidb[kk][:, ts * 128:(ts + 1) * 128], w2bb[kk])
                           for kk in range(8)]
                    self.mm_group(bank, mms, pre=[ev_w2, ev_hid])
                    self.evac(bank, 'vector',
                              lambda e, ts=ts, dsl=dsl, bank=bank:
                              e.tensor_tensor(
                                  self.x1[ts][:, dsl * 512:(dsl + 1) * 512],
                                  self.x1[ts][:, dsl * 512:(dsl + 1) * 512],
                                  bank.ap[:, 0:512], ALU.add))
        self.ev_l1ffn = self.evt("VE")
        self.dbg('out0', self.x1[0], [self.ev_l1ffn])
        self.dbg('out7', self.x1[7], [self.ev_l1ffn])
        sy.wait(self.evt("VE"))
        for ts in range(NCH):
            sy.op(lambda e, ts=ts: e.dma_start(
                out=self.out.ap()[ts * 128:(ts + 1) * 128, :],
                in_=self.x1[ts]), inc=16)

    def ph_dbg_stores(self):
        sy = self.E['sync']
        for t, ap, evs in self._dbg_jobs:
            sy.waits(evs)
            sy.op(lambda e, t=t, ap=ap: e.dma_start(out=t.ap(), in_=ap), inc=16)

    def ph_final_wait(self):
        sy = self.E['sync']
        for s in ("SL", "GL", "TE", "VE", "SE", "GP", "CC"):
            sy.wait(self.evt(s))


def build_nc(cfg):
    mk = MK(cfg)
    return mk.build(), mk


import ml_dtypes
from concourse.bass_utils import run_bass_kernel_spmd

BF = ml_dtypes.bfloat16
NEG = -1e30
T, D, NH, FFI = 1024, 2048, 8, 8192


def hadamard(n):
    H = np.array([[1.0]], np.float32)
    while H.shape[0] < n:
        H = np.block([[H, H], [H, -H]])
    return H.astype(np.float32)


def u8_of_int32(a):
    return a.view(np.uint8).reshape(*a.shape, 4)[..., 0]


def stage(d):
    C = np.ascontiguousarray
    sh = {}
    sh['WqT'] = C(d['L0_attn_Wq'].T)
    sh['WkT'] = C(d['L0_attn_Wk'].T)
    sh['WvT'] = C(d['L0_attn_Wv'].T)
    sh['WogT'] = C(d['L0_attn_Wog'].T)
    sh['WiT'] = C(d['L0_attn_Wig'].T)
    sh['WfT'] = C(d['L0_attn_Wfg'].T)
    sh['bi'] = C(d['L0_attn_big'][None, :])
    sh['bf'] = C(d['L0_attn_bfg'][None, :])
    sh['gam'] = C(d['L0_attn_gamma'].reshape(1, -1))
    sh['WoutT'] = C(d['L0_attn_Wout'].T)
    sh['QgT'] = C(u8_of_int32(d['L0_ffn_gate_w']).T)
    sh['QuT'] = C(u8_of_int32(d['L0_ffn_up_w']).T)
    sh['QdT'] = C(u8_of_int32(d['L0_ffn_down_w']).T)
    sh['gsc'] = C(d['L0_ffn_gate_scale'].reshape(64, 128).T)
    sh['usc'] = C(d['L0_ffn_up_scale'].reshape(64, 128).T)
    sh['dscr'] = C(d['L0_ffn_down_scale'][None, :])
    sh['wa'] = d['L1_attn_w_a']
    sh['wi'] = d['L1_attn_w_i']
    sh['wv'] = d['L1_attn_w_v']
    sh['opw'] = d['L1_attn_out_proj_w']
    wgu = d['L1_attn_w_gate_up']
    sh['wguI'] = C(wgu.reshape(D, 2, 64, 128).transpose(0, 2, 1, 3).reshape(D, 2 * FFI))
    sh['wdn'] = d['L1_attn_w_down']
    sh['dbias'] = C(d['L1_attn_decay_bias'].reshape(16, 128).T)
    sh['pnw'] = C(d['L1_attn_pre_norm_w'][None, :])
    sh['fnw'] = C(d['L1_attn_ffn_norm_w'][None, :])
    sh['pnwT'] = C(d['L1_attn_pre_norm_w'].reshape(16, 128).T)
    sh['fnwT'] = C(d['L1_attn_ffn_norm_w'].reshape(16, 128).T)
    sh['W1T'] = C(d['L1_ffn_W1'].T)
    sh['W3T'] = C(d['L1_ffn_W3'].T)
    w2t = C(d['L1_ffn_W2'].T)
    sh['W2Tp'] = C(w2t.reshape(64, 128, D).transpose(1, 0, 2).reshape(FFI, D))
    sh['ident'] = np.eye(128, dtype=BF)
    sh['ident_f'] = np.eye(128, dtype=np.float32)
    sh['H128'] = hadamard(128).astype(BF)
    sh['I2H64'] = np.kron(np.eye(2, dtype=np.float32), hadamard(64)).astype(BF)
    sh['triu01'] = np.triu(np.ones((128, 128), np.float32))
    m0 = np.ones((NH, T), np.float32); m0[:, 0::128] = 0.0
    sh['mask0_128'] = m0
    mn = np.zeros((NH, T), np.float32); mn[:, 0::128] = NEG
    sh['maskneg_128'] = mn
    m064 = np.ones((128, T), np.float32); m064[:, 0::64] = 0.0
    sh['mask0_64'] = m064
    m164 = np.zeros((128, T), np.float32); m164[:, 0::64] = 1.0
    sh['mask1_64'] = m164

    percore = []
    x = d['L0_attn_x']
    for c in range(8):
        b, sb = c // 4, c % 4
        pc = dict(sh)
        pc['x'] = C(x[b, sb * T:(sb + 1) * T, :])
        fsel = np.zeros((NH, 3), np.float32)
        fsel[:, :sb] = 1.0
        pc['fsel'] = fsel
        pc['fneg'] = (1.0 - fsel) * NEG
        f128 = np.zeros((128, 3), np.float32)
        f128[:, :sb] = 1.0
        pc['fsel128'] = f128
        pc['fselb128'] = 1.0 - f128
        percore.append(pc)
    return percore


_CFG = dict(phases={'kvif', 'scaffold', 'summary', 'qog', 'fold', 'scan2',
                    'wout', 'ffn0', 'l1pre', 'w2prep', 'l1attn', 'l1post',
                    'l1ffn'}, dbg=set())


def kernel(**inputs):
    percore = stage(inputs)
    nc, _mk = build_nc(_CFG)
    res = run_bass_kernel_spmd(nc, percore, core_ids=list(range(8)))
    full = np.empty((2, 4096, 2048), np.float32)
    for c in range(8):
        b, sb = c // 4, c % 4
        full[b, sb * 1024:(sb + 1) * 1024, :] = res.results[c]['out']
    return full
